# version 21
# baseline (speedup 1.0000x reference)
import numpy as np
import contextlib
import concourse.bass as bass
import concourse.mybir as mybir
from concourse.bass_utils import run_bass_kernel_spmd

F32, BF16 = mybir.dt.float32, mybir.dt.bfloat16
AF = mybir.ActivationFunctionType
ALU = mybir.AluOpType
AX = mybir.AxisListType

D = 1024
T = 2048
NH, NKV, HD = 16, 4, 64
NT = T // 128
NSEQ = 2
EPS = 1e-5
NCORES = 8

ENGINES = ("pe", "dve", "act", "sq", "gq")
DMA_ENGINES = ("sq", "gq")
DMA_KINDS = ("wls", "wlq", "wlo", "wlc", "gb", "rb",
             "wb0", "wb1", "wb2", "wb3", "wb4", "wb5", "wb6", "wb7",
             "xf0", "xf1", "xf2", "xb0", "xb1", "x1b", "yo0", "yo1", "dbg")
CLOSED_KINDS = ("wls", "wlq", "wlo", "wlc", "rb")


class Op:
    __slots__ = ("eng", "fn", "after", "referenced", "sem", "inc_no")

    def __init__(self, eng, fn, after, sem):
        self.eng, self.fn, self.after = eng, fn, after
        self.referenced = False
        self.sem = sem
        self.inc_no = None


class Prog:
    def __init__(self):
        self.ops = {e: [] for e in ENGINES}
        self.res_writer = {}
        self.res_readers = {}
        self.sem_total = {}
        self.pe_fifo = []
        self.defer = False
        self.wf_flag = False

    def add(self, eng, fn, reads=(), writes=(), after=(), dsem=None, direct=False):
        if eng in DMA_ENGINES:
            assert dsem in DMA_KINDS, f"dma op needs a kind, got {dsem}"
            sem = dsem
        else:
            sem = eng
        deps = list(after)
        for r in reads:
            w = self.res_writer.get(r)
            if w is not None:
                deps.append(w)
        for r in writes:
            w = self.res_writer.get(r)
            if w is not None:
                deps.append(w)
            deps.extend(self.res_readers.get(r, ()))
        op = Op(eng, fn, deps, sem)
        if eng == "pe" and self.defer and not direct:
            self.pe_fifo.append((op, self.wf_flag))
        else:
            self.ops[eng].append(op)
        for r in reads:
            self.res_readers.setdefault(r, []).append(op)
        for r in writes:
            self.res_writer[r] = op
            self.res_readers[r] = []
        return op

    def drain_pe(self, n=None, thru_wf=False):
        fifo = self.pe_fifo
        if thru_wf:
            last = -1
            for i_, (op_, wf_) in enumerate(fifo):
                if wf_:
                    last = i_
            take, self.pe_fifo = fifo[:last + 1], fifo[last + 1:]
        elif n is None:
            take, self.pe_fifo = fifo, []
        else:
            take, self.pe_fifo = fifo[:n], fifo[n:]
        for op_, wf_ in take:
            self.ops["pe"].append(op_)

    def number(self):
        for e in ENGINES:
            for op in self.ops[e]:
                for d in op.after:
                    if d is not op:
                        d.referenced = True
        cnt = {}
        for e in ENGINES:
            for op in self.ops[e]:
                if e in DMA_ENGINES:
                    cnt[op.sem] = cnt.get(op.sem, 0) + 16
                    op.inc_no = cnt[op.sem]
                elif op.referenced:
                    cnt[op.sem] = cnt.get(op.sem, 0) + 1
                    op.inc_no = cnt[op.sem]
        self.sem_total = cnt

    def emit(self, e, eng, sems):
        waited = {}
        for op in self.ops[e]:
            need = {}
            for d in op.after:
                tgt = d.inc_no
                if tgt is None:
                    continue
                if d.sem in CLOSED_KINDS:
                    tgt = self.sem_total[d.sem]
                if need.get(d.sem, 0) < tgt:
                    need[d.sem] = tgt
            for s_, tgt in need.items():
                if waited.get(s_, 0) < tgt:
                    eng.wait_ge(sems[s_], tgt)
                    waited[s_] = tgt
            inst = op.fn(eng)
            if op.inc_no is not None and (op.referenced or op.eng in DMA_ENGINES):
                inst.then_inc(sems[op.sem], 16 if op.eng in DMA_ENGINES else 1)

    def total(self, sem):
        return self.sem_total.get(sem, 0)


def build_nc(dbg=False, ntiles=NSEQ * NT):
    NTILES_RT = [ntiles]
    nc = bass.Bass("TRN2", target_bir_lowering=False, debug=False)

    x_d = nc.dram_tensor("x", [NSEQ, T, D], F32, kind="ExternalInput").ap()
    wqkv_d = nc.dram_tensor("wqkv", [D, 1536], F32, kind="ExternalInput").ap()
    wo_d = nc.dram_tensor("wo", [D, D], F32, kind="ExternalInput").ap()
    bqkv_d = nc.dram_tensor("bqkv", [1, 1536], F32, kind="ExternalInput").ap()
    bo_d = nc.dram_tensor("bo", [1, D], F32, kind="ExternalInput").ap()
    bffn_d = nc.dram_tensor("bffn", [1, D], F32, kind="ExternalInput").ap()
    g1r_d = nc.dram_tensor("g1r", [32, 128], F32, kind="ExternalInput").ap()
    g1i_d = nc.dram_tensor("g1i", [32, 128], F32, kind="ExternalInput").ap()
    g2r_d = nc.dram_tensor("g2r", [64, 16], F32, kind="ExternalInput").ap()
    g2i_d = nc.dram_tensor("g2i", [64, 16], F32, kind="ExternalInput").ap()
    eye_d = nc.dram_tensor("eye", [128, 128], F32, kind="ExternalInput").ap()
    ones_d = nc.dram_tensor("onesm", [128, 128], F32, kind="ExternalInput").ap()
    mask_d = nc.dram_tensor("maskc", [128, 128], F32, kind="ExternalInput").ap()
    cos_d = nc.dram_tensor("costab", [128, 512], F32, kind="ExternalInput").ap()
    sin_d = nc.dram_tensor("sintab", [128, 512], F32, kind="ExternalInput").ap()
    y_d = nc.dram_tensor("y", [NSEQ, T, D], F32, kind="ExternalOutput").ap()
    if dbg:
        dbg_t = {
            "d_xt": nc.dram_tensor("d_xt", [128, 1024], BF16, kind="ExternalOutput").ap(),
            "d_rot": nc.dram_tensor("d_rot", [128, 1280], BF16, kind="ExternalOutput").ap(),
            "d_cb": nc.dram_tensor("d_cb", [128, 1], F32, kind="ExternalOutput").ap(),
            "d_e0": nc.dram_tensor("d_e0", [128, 1024], BF16, kind="ExternalOutput").ap(),
            "d_e1": nc.dram_tensor("d_e1", [128, 1024], BF16, kind="ExternalOutput").ap(),
            "d_ctx": nc.dram_tensor("d_ctx", [128, 1024], BF16, kind="ExternalOutput").ap(),
            "d_x1": nc.dram_tensor("d_x1", [128, 1024], F32, kind="ExternalOutput").ap(),
            "d_qt": nc.dram_tensor("d_qt", [128, 1024], BF16, kind="ExternalOutput").ap(),
            "d_wffn": nc.dram_tensor("d_wffn", [128, 8192], BF16, kind="ExternalOutput").ap(),
            "d_corrq": nc.dram_tensor("d_corrq", [2, 1536], BF16, kind="ExternalOutput").ap(),
            "d_corrf": nc.dram_tensor("d_corrf", [2, 1024], BF16, kind="ExternalOutput").ap(),
            "d_vs": nc.dram_tensor("d_vs", [128, 260], BF16, kind="ExternalOutput").ap(),
            "d_kt": nc.dram_tensor("d_kt", [128, 256], BF16, kind="ExternalOutput").ap(),
        }
    d2r_d = nc.dram_tensor("d2r", [512, 512], BF16, kind="Internal")
    d2i_d = nc.dram_tensor("d2i", [512, 512], BF16, kind="Internal")

    with contextlib.ExitStack() as st:
        def sb(name, shape, dt):
            return st.enter_context(nc.sbuf_tensor(name, shape, dt))

        wqkv = sb("wqkv_s", [128, 8 * 1536], BF16)
        wo = sb("wo_s", [128, 8 * 1024], BF16)
        wffn = sb("wffn_s", [128, 8 * 1024], BF16)
        kt = sb("kt_s", [128, 2 * NT * 128], BF16)
        vsb = sb("v_s", [128, NT * 260], BF16)
        esb = sb("e_s", [128, NT * 8 * 128], BF16)
        cost = sb("cos_s", [128, 512], F32)
        sint = sb("sin_s", [128, 512], F32)
        eyef = sb("eyef", [128, 128], F32)
        eyeb = sb("eyeb", [128, 128], BF16)
        maskb = sb("maskb", [128, 128], BF16)
        onesb = sb("onesb", [128, 128], BF16)
        xf = [sb(f"xf{k}", [128, D], F32) for k in range(3)]
        xb = [sb(f"xb{k}", [128, D], BF16) for k in range(2)]
        xt = sb("xt_s", [128, D], BF16)
        x1f = sb("x1f", [128, D], F32)
        x1b = sb("x1b", [128, D], BF16)
        x1t = sb("x1t", [128, D], BF16)
        x2f = [sb(f"x2f{k}", [128, D], F32) for k in range(2)]
        pa = sb("pa", [128, 640], F32)
        pb = sb("pb", [128, 640], F32)
        pc = sb("pcn", [128, 640], F32)
        pd = sb("pd", [128, 640], F32)
        rot = sb("rot", [128, 1280], BF16)
        qt = [sb(f"qt_s{k}", [128, D], BF16) for k in range(2)]
        ctxs = sb("ctxs", [128, D], BF16)
        ctxt = sb("ctxt", [128, D], BF16)
        corrq = sb("corrq", [2, 1536], BF16)
        corrf = sb("corrf", [2, 1024], BF16)
        bob = sb("bob", [1, 1024], BF16)
        srt1 = sb("srt1", [2, 128], BF16)
        srt2 = sb("srt2", [2, 128], BF16)
        st12 = sb("st12", [128, 12], F32)
        mv = sb("mv", [128, 10], F32)
        sc = sb("sc", [128, 16], F32)
        srow1 = sb("srow1", [128, 2], BF16)
        srow1b = sb("srow1b", [128, 2], BF16)
        srow2 = sb("srow2", [128, 2], BF16)
        qkp = sb("qkp", [128, 2], F32)
        qkrow = sb("qkrow", [2, 128], F32)
        qkmax = sb("qkmax", [2, 1], F32)
        prow = sb("prow", [1, 2], F32)
        krun = sb("krun", [1, 1], F32)
        c8n = sb("c8n", [1, 1], BF16)
        cbias = [sb(f"cbias{k}", [128, 1], F32) for k in range(2)]
        epsc = sb("epsc", [128, 1], F32)
        recipd = sb("recipd", [128, 8], F32)
        sqs = sb("sqs", [128, 1024], F32)
        g1t = sb("g1t", [32, 128], BF16)
        g2t = sb("g2t", [64, 16], BF16)
        ta = sb("ta", [128, 32], BF16)
        tg = sb("tg", [16, 64], BF16)
        avec = sb("avec", [4, 1024], BF16)
        bvec = sb("bvec", [4, 256], BF16)
        wst = [sb(f"wst{k}", [128, 256], BF16) for k in range(8)]
        g1u = sb("g1u", [32, 128], BF16)
        g2u = sb("g2u", [64, 16], BF16)

        qkv_ps = st.enter_context(nc.psum_tensor("qkv_ps", [128, 1536], F32))
        wf_ps = st.enter_context(nc.psum_tensor("wf_ps", [128, 1024], F32))
        c_ps = st.enter_context(nc.psum_tensor("c_ps", [128, 1024], F32))
        t_ps = st.enter_context(nc.psum_tensor("t_ps", [128, 1024], BF16))

        ktv = kt[:].rearrange("p (c j t) -> p c j t", c=2, j=NT)

        sems = {e: st.enter_context(nc.semaphore(f"sem_{e}")) for e in ("pe", "dve", "act")}
        for kk_ in DMA_KINDS:
            sems[kk_] = st.enter_context(nc.semaphore(f"sem_{kk_}"))
        block = st.enter_context(nc.Block())

        P = Prog()

        P.add("sq", lambda e: e.dma_start(out=eyef[:], in_=eye_d), writes=["eyef"], dsem="wls")
        P.add("sq", lambda e: e.dma_start(out=cost[:], in_=cos_d), writes=["cos"], dsem="wls")
        P.add("sq", lambda e: e.dma_start(out=sint[:], in_=sin_d), writes=["sin"], dsem="wls")
        P.add("gq", lambda e: e.dma_start(out=eyeb[:], in_=eye_d), writes=["eyeb"], dsem="wlc")
        P.add("gq", lambda e: e.dma_start(out=maskb[:], in_=mask_d), writes=["maskb"], dsem="wlc")
        P.add("gq", lambda e: e.dma_start(out=onesb[:], in_=ones_d), writes=["onesb"], dsem="wlc")
        P.add("gq", lambda e: e.dma_start(out=bob[:], in_=bo_d), writes=["bob"], dsem="wlc")
        P.add("gq", lambda e: e.dma_start(out=corrq[1:2, :], in_=bqkv_d), writes=["corrq_b"], dsem="wlc")
        P.add("gq", lambda e: e.dma_start(out=corrf[1:2, :], in_=bffn_d), writes=["corrf_b"], dsem="wlc")
        P.add("gq", lambda e: e.dma_start(out=g1t[:], in_=g1r_d), writes=["g1"], dsem="wlc")
        P.add("gq", lambda e: e.dma_start(out=g2t[:], in_=g2r_d), writes=["g2"], dsem="wlc")
        P.add("gq", lambda e: e.dma_start(out=g1u[:], in_=g1i_d), writes=["g1i"], dsem="wlc")
        P.add("gq", lambda e: e.dma_start(out=g2u[:], in_=g2i_d), writes=["g2i"], dsem="wlc")
        for c in range(8):
            P.add("gq", (lambda cc: lambda e: e.dma_start(
                out=wqkv[:, cc * 1536:(cc + 1) * 1536],
                in_=wqkv_d[cc * 128:(cc + 1) * 128, :]))(c), writes=[f"wqkv{c}"], dsem="wlq")
        for c in range(8):
            P.add("gq", (lambda cc: lambda e: e.dma_start(
                out=wo[:, cc * 1024:(cc + 1) * 1024],
                in_=wo_d[cc * 128:(cc + 1) * 128, :]))(c), writes=[f"wo{c}"], dsem="wlo")
        P.add("dve", lambda e: e.memset(
            bass.AP(vsb, 64, [[NT * 260, 128], [65, 4 * NT]]), 1.0), writes=["vones"] + [f"vs{j}" for j in range(NT)])
        P.add("dve", lambda e: e.memset(epsc[:], EPS), writes=["epsc"])

        wq_deps = [f"wqkv{c}" for c in range(8)]
        wffn_ready = ([f"wfA{c}" for c in range(4)] + [f"wfB{c}" for c in range(4)] +
                      [f"wfN{c}" for c in range(4)] + [f"wfD{c}" for c in range(4)])

        def weight_units():
            def colsumQ():
                for nb in range(3):
                    for c in range(8):
                        P.add("pe", (lambda nb_, c_: lambda e: e.matmul(
                            c_ps[0:1, 0:512], onesb[:, 0:1],
                            wqkv[:, c_ * 1536 + nb_ * 512: c_ * 1536 + (nb_ + 1) * 512],
                            start=(c_ == 0), stop=(c_ == 7)))(nb, c),
                            reads=wq_deps + ["onesb"],
                            writes=(["cps", "cpsA"] if c in (0, 7) else []))
                    P.add("dve", (lambda nb_: lambda e: e.tensor_copy(
                        out=corrq[0:1, nb_ * 512:(nb_ + 1) * 512], in_=c_ps[0:1, 0:512]))(nb),
                        reads=["cps", "cpsA"], writes=[f"corrq_cs{nb}"])

            def build_half(d2_t, half, gs1, gs2, l1, l2):
                prev = ([f"av{c}" for c in range(4)] + [f"bv{c}" for c in range(4)]) if half else []
                P.add("pe", (lambda g_: lambda e: e.transpose(t_ps[:, 0:32], g_[:], eyeb[0:32, 0:32]))(gs1),
                      reads=[l1, "eyeb"], writes=["t_ps"])
                P.add("dve", (lambda pv: lambda e: e.tensor_copy(out=ta[:], in_=t_ps[:, 0:32]))(prev),
                      reads=["t_ps"] + prev, writes=["ta"])
                P.add("pe", (lambda g_: lambda e: e.transpose(t_ps[0:16, 32:96], g_[:], eyeb[0:64, 0:64]))(gs2),
                      reads=[l2, "eyeb", "ta"], writes=["t_ps"])
                P.add("dve", (lambda pv: lambda e: e.tensor_copy(out=tg[:], in_=t_ps[0:16, 32:96]))(prev),
                      reads=["t_ps"] + prev, writes=["tg"])
                for c in range(4):
                    P.add("sq", (lambda c_: lambda e: e.dma_start(
                        out=avec[c_:c_ + 1, :], in_=ta[c_ * 32:(c_ + 1) * 32, :]))(c),
                        reads=["ta"] + prev, writes=[f"av{c}"], dsem="gb")
                    P.add("sq", (lambda c_: lambda e: e.dma_start(
                        out=bvec[c_:c_ + 1, :], in_=tg[:, c_ * 16:(c_ + 1) * 16]))(c),
                        reads=["tg"] + prev, writes=[f"bv{c}"], dsem="gb")
                vdeps = [f"av{c}" for c in range(4)] + [f"bv{c}" for c in range(4)]
                for m in range(8):
                    poff = 512 + 256 * (m % 2)
                    P.add("pe", (lambda m_, o_: lambda e: e.matmul(
                        c_ps[:, o_:o_ + 256], avec[:, m_ * 128:(m_ + 1) * 128], bvec[:],
                        start=True, stop=True))(m, poff),
                        reads=vdeps, writes=[f"cps{m % 2}w", "cps", "cpsB"])
                    P.add("dve", (lambda b_, o_: lambda e: e.tensor_copy(
                        out=wst[b_][:], in_=c_ps[:, o_:o_ + 256]))(m, poff),
                        reads=[f"cps{m % 2}w", "cpsB"], writes=[f"wst{m}", f"cps{m % 2}w", "cps"])
                    for gh in range(2):
                        for dd in range(4):
                            def mk(m_, gh_, dd_, dt_):
                                def f(e):
                                    in_ap = bass.AP(wst[m_], dd_ * 32 * 256 + gh_ * 128,
                                                    [[256, 32], [16, 8], [1, 16]])
                                    out_ap = bass.AP(dt_, (m_ * 4 + dd_) * 8192 + gh_ * 512 * 8,
                                                     [[16, 32], [512, 8], [1, 16]])
                                    return e.dma_start(out=out_ap, in_=in_ap)
                                return f
                            P.add("sq", mk(m, gh, dd, d2_t),
                                  reads=[f"wst{m}"], writes=[f"d2dma{m}"], dsem=f"wb{m}")

            def h1():
                build_half(d2r_d, 0, g1t, g2t, "g1", "g2")

            def h2():
                build_half(d2i_d, 1, g1u, g2u, "g1i", "g2i")

            def wfin():
                rb_deps = [f"d2dma{m}" for m in range(8)]
                for c in range(4):
                    P.add("sq", (lambda c_: lambda e: e.dma_start(
                        out=wffn[:, c_ * 1024: c_ * 1024 + 512],
                        in_=bass.AP(d2r_d, c_ * 128 * 512, [[512, 128], [1, 512]])))(c),
                        reads=rb_deps, writes=[f"wfA{c}"], dsem="rb")
                    P.add("sq", (lambda c_: lambda e: e.dma_start(
                        out=wffn[:, c_ * 1024 + 512: (c_ + 1) * 1024],
                        in_=bass.AP(d2i_d, c_ * 128 * 512, [[512, 128], [1, 512]])))(c),
                        reads=rb_deps, writes=[f"wfB{c}"], dsem="rb")
                    P.add("sq", (lambda c_: lambda e: e.dma_start(
                        out=wffn[:, (4 + c_) * 1024: (4 + c_) * 1024 + 512],
                        in_=bass.AP(d2i_d, c_ * 128 * 512, [[512, 128], [1, 512]])))(c),
                        reads=rb_deps, writes=[f"wfC{c}"], dsem="rb")
                    P.add("sq", (lambda c_: lambda e: e.dma_start(
                        out=wffn[:, (4 + c_) * 1024 + 512: (5 + c_) * 1024],
                        in_=bass.AP(d2r_d, c_ * 128 * 512, [[512, 128], [1, 512]])))(c),
                        reads=rb_deps, writes=[f"wfD{c}"], dsem="rb")
                for c in range(4):
                    P.add("dve", (lambda c_: lambda e: e.tensor_scalar(
                        out=wffn[:, (4 + c_) * 1024: (4 + c_) * 1024 + 512],
                        in0=wffn[:, (4 + c_) * 1024: (4 + c_) * 1024 + 512],
                        scalar1=-1.0, scalar2=None, op0=ALU.mult))(c),
                        reads=[f"wfC{c}"], writes=[f"wfN{c}"])
                for nb in range(2):
                    for c in range(8):
                        P.add("pe", (lambda nb_, c_: lambda e: e.matmul(
                            c_ps[0:1, 0:512], onesb[:, 0:1],
                            wffn[:, c_ * 1024 + nb_ * 512: c_ * 1024 + (nb_ + 1) * 512],
                            start=(c_ == 0), stop=(c_ == 7)))(nb, c),
                            reads=wffn_ready + ["onesb"],
                            writes=(["cps", "cpsA"] if c in (0, 7) else []))
                    P.add("dve", (lambda nb_: lambda e: e.tensor_copy(
                        out=corrf[0:1, nb_ * 512:(nb_ + 1) * 512], in_=c_ps[0:1, 0:512]))(nb),
                        reads=["cps", "cpsA"], writes=[f"corrf_cs{nb}"])
                if dbg:
                    P.add("sq", lambda e: e.dma_start(out=dbg_t["d_corrf"], in_=corrf[:]),
                          reads=corrf_deps, dsem="dbg", writes=["dbgchain"])
                    P.add("sq", lambda e: e.dma_start(out=dbg_t["d_wffn"], in_=wffn[:]),
                          reads=wffn_ready, dsem="dbg", writes=["dbgchain"])

            return [colsumQ, h1, h2, wfin]

        corrq_deps = ["corrq_cs0", "corrq_cs1", "corrq_cs2", "corrq_b"]
        corrf_deps = ["corrf_cs0", "corrf_cs1", "corrf_b"]


        def emit_LN1(g):
            bxf = g % 3
            par = g % 2
            XF = f"xf{bxf}"
            mv1c = 8 * par
            l1c = 0 + 13 * par
            r1c = 1 + 13 * par
            sr1 = srow1b if par else srow1
            P.add("dve", (lambda x_: lambda e: e.bn_stats(out=st12[:, 0:6], in_=x_[:, 0:512]))(xf[bxf]),
                  reads=[XF], writes=["st12"])
            P.add("dve", (lambda x_: lambda e: e.bn_stats(out=st12[:, 6:12], in_=x_[:, 512:1024]))(xf[bxf]),
                  reads=[XF], writes=["st12"])
            P.add("dve", (lambda mc: lambda e: e.bn_aggr(out=mv[:, mc:mc + 2], in_=st12[:]))(mv1c),
                  reads=["st12"], writes=[f"mv1{par}"])
            P.add("act", (lambda mc, lc: lambda e: e.activation(
                out=sc[:, lc:lc + 1], in_=mv[:, mc + 1:mc + 2], func=AF.Ln, bias=epsc[:, 0:1]))(mv1c, l1c),
                reads=["epsc", f"mv1{par}"], writes=[f"lnv1{par}"])
            P.add("act", (lambda lc, rc: lambda e: e.activation(
                out=sc[:, rc:rc + 1], in_=sc[:, lc:lc + 1], func=AF.Exp, scale=-0.5))(l1c, r1c),
                reads=[f"lnv1{par}"], writes=[f"r1{par}"])
            P.add("act", (lambda lc, s_: lambda e: e.activation(
                out=s_[:, 1:2], in_=sc[:, lc:lc + 1], func=AF.Exp, scale=0.5))(l1c, sr1),
                reads=[f"lnv1{par}"], writes=[f"srow1{par}"])
            P.add("dve", (lambda mc, s_: lambda e: e.tensor_scalar(
                out=s_[:, 0:1], in0=mv[:, mc:mc + 1],
                scalar1=-1.0, scalar2=None, op0=ALU.mult))(mv1c, sr1),
                reads=[f"mv1{par}"], writes=[f"srow1{par}"])

        def emit_epiA1(p, first):
            ps, pi, pbx = p
            for c in range(8):
                hdeps = [f"ctxs{c // 4}_{g_}" for g_ in range(2)]
                P.add("pe", (lambda c_: lambda e: e.transpose(
                    t_ps[:, c_ * 128:(c_ + 1) * 128], ctxs[:, c_ * 128:(c_ + 1) * 128], eyeb[:]))(c),
                    reads=hdeps + ["eyeb"], writes=(["t_ps"] if c in (0, 7) else []), direct=True)
            P.add("dve", lambda e: e.tensor_copy(out=ctxt[:, 0:512], in_=t_ps[:, 0:512]),
                  reads=["t_ps"], writes=["ctxt"])
            P.add("dve", lambda e: e.tensor_copy(out=ctxt[:, 512:1024], in_=t_ps[:, 512:1024]),
                  reads=["t_ps"], writes=["ctxt"])
            P.wf_flag = True
            for nb in range(2):
                for c in range(8):
                    P.add("pe", (lambda nb_, c_: lambda e: e.matmul(
                        wf_ps[:, nb_ * 512:(nb_ + 1) * 512],
                        ctxt[:, c_ * 128:(c_ + 1) * 128],
                        wo[:, c_ * 1024 + nb_ * 512: c_ * 1024 + (nb_ + 1) * 512],
                        start=(c_ == 0), stop=False))(nb, c),
                        reads=["ctxt"] + ([f"wo{cc}" for cc in range(8)] if first else []),
                        writes=(["wf"] if (nb, c) == (0, 0) else []))
                P.add("pe", (lambda nb_: lambda e: e.matmul(
                    wf_ps[:, nb_ * 512:(nb_ + 1) * 512], onesb[0:1, :],
                    bob[0:1, nb_ * 512:(nb_ + 1) * 512], start=False, stop=True))(nb),
                    reads=["bob", "onesb"], writes=(["wf"] if nb == 1 else []))
            P.wf_flag = False

        def emit_epiA2(p, first):
            ps, pi, pbx = p
            pbxf = (ps * NT + pi) % 3
            PXF = f"xf{pbxf}"
            ctx_deps = [f"ctxs{H2}_{g_}" for H2 in range(2) for g_ in range(2)]
            P.add("dve", lambda e: e.bn_stats(out=st12[:, 0:6], in_=wf_ps[:, 0:512]),
                  reads=["wf"], writes=["st12"])
            P.add("dve", lambda e: e.bn_stats(out=st12[:, 6:12], in_=wf_ps[:, 512:1024]),
                  reads=["wf"], writes=["st12"])
            P.add("dve", lambda e: e.bn_aggr(out=mv[:, 2:4], in_=st12[:]),
                  reads=["st12"], writes=["mv2"])
            P.add("act", lambda e: e.activation(out=sc[:, 2:3], in_=mv[:, 3:4], func=AF.Ln, bias=epsc[:, 0:1]),
                  reads=["epsc", "mv2"], writes=["lnv2"])
            P.add("act", lambda e: e.activation(out=sc[:, 3:4], in_=sc[:, 2:3], func=AF.Exp, scale=-0.5),
                  reads=["lnv2"], writes=["r2"])
            P.add("dve", lambda e: e.tensor_tensor(out=sc[:, 4:5], in0=mv[:, 2:3], in1=sc[:, 3:4],
                                                   op=ALU.mult),
                  reads=["mv2", "r2"], writes=["mr2"])
            P.add("dve", (lambda x_: lambda e: e.tensor_scalar(
                out=x1f[:], in0=x_[:], scalar1=sc[:, 4:5], scalar2=None, op0=ALU.subtract))(xf[pbxf]),
                reads=[PXF, "mr2"], writes=["x1f"])
            P.add("dve", lambda e: e.scalar_tensor_tensor(out=x1f[:], in0=wf_ps[:, 0:1024],
                                                          scalar=sc[:, 3:4], in1=x1f[:],
                                                          op0=ALU.mult, op1=ALU.add),
                  reads=["wf", "r2", "x1f"], writes=["x1f"])
            P.add("dve", lambda e: e.tensor_copy(out=x1b[:], in_=x1f[:]),
                  reads=["x1f"], writes=["x1b"])
            P.add("dve", lambda e: e.bn_stats(out=st12[:, 0:6], in_=x1f[:, 0:512]),
                  reads=["x1f"], writes=["st12"])
            P.add("dve", lambda e: e.bn_stats(out=st12[:, 6:12], in_=x1f[:, 512:1024]),
                  reads=["x1f"], writes=["st12"])
            P.add("dve", lambda e: e.bn_aggr(out=mv[:, 4:6], in_=st12[:]),
                  reads=["st12"], writes=["mv3"])
            P.add("act", lambda e: e.activation(out=sc[:, 5:6], in_=mv[:, 5:6], func=AF.Ln, bias=epsc[:, 0:1]),
                  reads=["epsc", "mv3"], writes=["lnv3"])
            P.add("act", lambda e: e.activation(out=sc[:, 6:7], in_=sc[:, 5:6], func=AF.Exp, scale=-0.5),
                  reads=["lnv3"], writes=["r3"])
            P.add("act", lambda e: e.activation(out=srow2[:, 1:2], in_=sc[:, 5:6], func=AF.Exp, scale=0.5),
                  reads=["lnv3"], writes=["srow2"])
            P.add("dve", lambda e: e.tensor_scalar(out=srow2[:, 0:1], in0=mv[:, 4:5],
                                                   scalar1=-1.0, scalar2=None, op0=ALU.mult),
                  reads=["mv3"], writes=["srow2"])
            if dbg and (ps, pi) == (0, 0):
                P.add("sq", lambda e: e.dma_start(out=dbg_t["d_ctx"], in_=ctxs[:]),
                      reads=ctx_deps, dsem="dbg", writes=["dbgchain"])
                P.add("sq", lambda e: e.dma_start(out=dbg_t["d_x1"], in_=x1f[:]),
                      reads=["x1f"], dsem="dbg", writes=["dbgchain"])

        def emit_P1(s, i, bx, first, skip_xt=False):
            par = (s * NT + i) % 2
            sr1 = srow1b if par else srow1
            P.add("pe", (lambda s_: lambda e: e.transpose(t_ps[0:2, 0:128], s_[:], eyeb[:]))(sr1),
                  reads=[f"srow1{par}", "eyeb"], writes=["t_ps"], direct=True)
            P.add("dve", lambda e: e.tensor_copy(out=srt1[:], in_=t_ps[0:2, 0:128]),
                  reads=["t_ps"], writes=["srt1"])
            for nb in range(3):
                P.add("pe", (lambda nb_: lambda e: e.matmul(
                    qkv_ps[:, nb_ * 512:(nb_ + 1) * 512], srt1[:],
                    corrq[:, nb_ * 512:(nb_ + 1) * 512], start=True, stop=False))(nb),
                    reads=["srt1"] + (corrq_deps if first else []),
                    writes=(["qkv"] if nb == 0 else []))
            for nb in range(3):
                for c in range(8):
                    P.add("pe", (lambda nb_, c_: lambda e: e.matmul(
                        qkv_ps[:, nb_ * 512:(nb_ + 1) * 512],
                        xt[:, c_ * 128:(c_ + 1) * 128],
                        wqkv[:, c_ * 1536 + nb_ * 512: c_ * 1536 + (nb_ + 1) * 512],
                        start=False, stop=(c_ == 7)))(nb, c),
                        reads=["xt"] + (wq_deps if first else []),
                        writes=(["qkv"] if (nb, c) == (2, 7) else []))

        def emit_P1b(s, i, bx, first):
            par = (s * NT + i) % 2
            r1c = 1 + 13 * par
            qe = bass.AP(qkv_ps, 0, [[1536, 128], [64, 20], [2, 32]])
            qo = bass.AP(qkv_ps, 1, [[1536, 128], [64, 20], [2, 32]])
            cosv = bass.AP(cost, i * 32, [[512, 128], [0, 20], [1, 32]])
            sinv = bass.AP(sint, i * 32, [[512, 128], [0, 20], [1, 32]])
            for dst, srcv, tab, nm in ((pa, qe, cosv, "pa"), (pb, qo, sinv, "pb"),
                                       (pc, qe, sinv, "pcn"), (pd, qo, cosv, "pd")):
                P.add("dve", (lambda d_, s_, t_, rc: lambda e: e.scalar_tensor_tensor(
                    out=d_[:], in0=s_, scalar=sc[:, rc:rc + 1], in1=t_,
                    op0=ALU.mult, op1=ALU.mult))(dst, srcv, tab, r1c),
                    reads=["qkv", f"r1{par}", "cos", "sin"], writes=[nm])
            for Hh in range(2):
                for off, iA, iB, rA, rB in ((0, pa, pb, "pa", "pb"), (32, pc, pd, "pcn", "pd")):
                    P.add("dve", (lambda H_, o_, a_, b_: lambda e: e.tensor_tensor(
                        out=bass.AP(rot, 512 * H_ + o_, [[1280, 128], [128, 4], [64, 2], [1, 32]]),
                        in0=bass.AP(a_, 256 * H_, [[640, 128], [32, 4], [128, 2], [1, 32]]),
                        in1=bass.AP(b_, 256 * H_, [[640, 128], [32, 4], [128, 2], [1, 32]]),
                        op=ALU.subtract))(Hh, off, iA, iB),
                        reads=[rA, rB], writes=["rot"])
            for off, iA, iB, rA, rB in ((0, pa, pb, "pa", "pb"), (32, pc, pd, "pcn", "pd")):
                P.add("dve", (lambda o_, a_, b_: lambda e: e.tensor_tensor(
                    out=bass.AP(rot, 1024 + o_, [[1280, 128], [64, 4], [1, 32]]),
                    in0=bass.AP(a_, 512, [[640, 128], [32, 4], [1, 32]]),
                    in1=bass.AP(b_, 512, [[640, 128], [32, 4], [1, 32]]),
                    op=ALU.subtract))(off, iA, iB),
                    reads=[rA, rB], writes=["rot"])
            P.add("dve", (lambda i_, rc: lambda e: e.tensor_scalar(
                out=bass.AP(vsb, i_ * 260, [[NT * 260, 128], [65, 4], [1, 64]]),
                in0=bass.AP(qkv_ps, 1280, [[1536, 128], [64, 4], [1, 64]]),
                scalar1=sc[:, rc:rc + 1], scalar2=None, op0=ALU.mult))(i, r1c),
                reads=["qkv", f"r1{par}"] + (["vones"] if first else []), writes=[f"vs{i}"])
        def emit_xt(bx):
            XB = f"xb{bx}"
            for c in range(8):
                P.add("pe", (lambda c_, x_: lambda e: e.transpose(
                    t_ps[:, c_ * 128:(c_ + 1) * 128],
                    x_[:, c_ * 128:(c_ + 1) * 128], eyeb[:]))(c, xb[bx]),
                    reads=[XB, "eyeb"], writes=(["t_ps"] if c in (0, 7) else []))
            P.add("dve", lambda e: e.tensor_copy(out=xt[:], in_=t_ps[:]),
                  reads=["t_ps"], writes=["xt"])

        def emit_P2a(s, i, bx):
            gpar = (s * NT + i) % 2
            P.add("act", lambda e: e.activation(out=sqs[:], in_=rot[:, 0:1024],
                                                func=AF.Square, accum_out=qkp[:, 0:1]),
                  reads=["rot"], writes=["qkp", "sqs"])
            P.add("act", lambda e: e.activation(out=sqs[:, 0:256], in_=rot[:, 1024:1280],
                                                func=AF.Square, accum_out=qkp[:, 1:2]),
                  reads=["rot"], writes=["qkp", "sqs"])
            for cc in range(8):
                P.add("pe", (lambda cc_: lambda e: e.transpose(
                    t_ps[:, cc_ * 128:(cc_ + 1) * 128],
                    rot[:, cc_ * 128:(cc_ + 1) * 128], eyeb[:]))(cc),
                    reads=["rot", "eyeb"], writes=(["t_ps"] if cc in (0, 7) else []))
            P.add("dve", (lambda g_: lambda e: e.tensor_copy(out=qt[g_][:], in_=t_ps[:]))(gpar),
                  reads=["t_ps"], writes=[f"qt{gpar}"])
            for kc in range(2):
                P.add("pe", (lambda kc_: lambda e: e.transpose(
                    t_ps[:, kc_ * 128:(kc_ + 1) * 128],
                    rot[:, 1024 + kc_ * 128:1024 + (kc_ + 1) * 128], eyeb[:]))(kc),
                    reads=["rot", "eyeb"], writes=["t_ps"])
            P.add("dve", (lambda i_: lambda e: e.tensor_copy(
                out=bass.AP(kt, i_ * 128, [[2 * NT * 128, 128], [NT * 128, 2], [1, 128]]),
                in_=t_ps[:, 0:256].rearrange("p (c t) -> p c t", c=2)))(i),
                reads=["t_ps"], writes=[f"kt{i}"])
            P.add("pe", lambda e: e.transpose(qkv_ps[0:2, 1400:1528], qkp[:], eyef[:]),
                  reads=["qkp", "eyef"], writes=["qkv"])
            P.add("dve", lambda e: e.tensor_copy(out=qkrow[:], in_=qkv_ps[0:2, 1400:1528]),
                  reads=["qkv"], writes=["qkrow"])
            P.add("dve", lambda e: e.tensor_reduce(out=qkmax[:], in_=qkrow[:], axis=AX.X, op=ALU.max),
                  reads=["qkrow"], writes=["qkmax"])
            P.add("pe", lambda e: e.transpose(qkv_ps[0:1, 1398:1400], qkmax[:], eyef[0:2, 0:2]),
                  reads=["qkmax", "eyef"], writes=["qkv"])
            P.add("dve", lambda e: e.tensor_copy(out=prow[:], in_=qkv_ps[0:1, 1398:1400]),
                  reads=["qkv"], writes=["prow"])

        def emit_P2b(s, i, bx):
            gpar = (s * NT + i) % 2
            if i == 0:
                P.add("dve", lambda e: e.tensor_copy(out=krun[:], in_=prow[0:1, 1:2]),
                      reads=["prow"], writes=["krun"])
            else:
                P.add("dve", lambda e: e.tensor_tensor(out=krun[:], in0=krun[:],
                                                       in1=prow[0:1, 1:2], op=ALU.max),
                      reads=["prow"], writes=["krun"])
            P.add("dve", lambda e: e.tensor_tensor(out=prow[0:1, 0:1], in0=prow[0:1, 0:1],
                                                   in1=krun[:], op=ALU.mult),
                  reads=["krun"], writes=["prow"])
            P.add("act", lambda e: e.activation(out=prow[0:1, 1:2], in_=prow[0:1, 0:1],
                                                func=AF.Ln, bias=epsc[0:1, 0:1]),
                  reads=["prow", "epsc"], writes=["prowln"])
            P.add("act", lambda e: e.activation(out=prow[0:1, 0:1], in_=prow[0:1, 1:2],
                                                func=AF.Exp, scale=0.5),
                  reads=["prowln"], writes=["prow"])
            P.add("dve", lambda e: e.tensor_scalar(out=c8n[:], in0=prow[0:1, 0:1],
                                                   scalar1=-0.125, scalar2=None, op0=ALU.mult),
                  reads=["prow"], writes=["c8n"])

        def emit_P2c(s, i, bx):
            gpar = (s * NT + i) % 2
            P.add("pe", lambda e: e.matmul(qkv_ps[:, 1535:1536], onesb[0:1, :], c8n[:],
                                           start=True, stop=True),
                  reads=["c8n", "onesb"], writes=["qkv"]),
            P.add("dve", (lambda g_: lambda e: e.tensor_copy(out=cbias[g_][:], in_=qkv_ps[:, 1535:1536]))(gpar),
                  reads=["qkv"], writes=[f"cbias{gpar}"])
            gidx_ = s * NT + i
            if gidx_ + 2 < NTILES_RT[0]:
                s2, i2 = divmod(gidx_ + 2, NT)
                P.add("gq", (lambda s_, i_, b_: lambda e: e.dma_start(
                    out=xb[b_][:], in_=x_d[s_, i_ * 128:(i_ + 1) * 128, :]))(s2, i2, gpar),
                    writes=[f"xb{gpar}"], dsem=f"xb{gpar}")
            if dbg and (s, i) == (0, 0):
                P.add("sq", lambda e: e.dma_start(out=dbg_t["d_xt"], in_=xt[:]), reads=["xt"], dsem="dbg", writes=["dbgchain"])
                P.add("sq", lambda e: e.dma_start(out=dbg_t["d_rot"], in_=rot[:]), reads=["rot"], dsem="dbg", writes=["dbgchain"])
                P.add("sq", lambda e: e.dma_start(out=dbg_t["d_cb"], in_=cbias[0][:]), reads=["cbias0"], dsem="dbg", writes=["dbgchain"])
                P.add("sq", lambda e: e.dma_start(out=dbg_t["d_qt"], in_=qt[0][:]), reads=["qt0"], dsem="dbg", writes=["dbgchain"])
                P.add("sq", lambda e: e.dma_start(out=dbg_t["d_vs"], in_=vsb[:, 0:260]), reads=["vs0"], dsem="dbg", writes=["dbgchain"])
                P.add("sq", lambda e: e.dma_start(out=dbg_t["d_kt"][:, 0:128], in_=kt[:, 0:128]), reads=["kt0"], dsem="dbg", writes=["dbgchain"])
                P.add("sq", lambda e: e.dma_start(out=dbg_t["d_kt"][:, 128:256], in_=kt[:, 2048:2176]), reads=["kt0"], dsem="dbg", writes=["dbgchain"])
                P.add("sq", lambda e: e.dma_start(out=dbg_t["d_corrq"], in_=corrq[:]),
                      reads=corrq_deps, dsem="dbg", writes=["dbgchain"])

        def emit_epiB1a(p, first):
            ps, pi, pbx = p
            pg = ps * NT + pi
            if pg + 3 < NTILES_RT[0]:
                s3, i3 = divmod(pg + 3, NT)
                b3 = (pg + 3) % 3
                P.add("sq", (lambda s_, i_, b_: lambda e: e.dma_start(
                    out=xf[b_][:], in_=x_d[s_, i_ * 128:(i_ + 1) * 128, :]))(s3, i3, b3),
                    writes=[f"xf{b3}"], dsem=f"xf{b3}")
            for c in range(8):
                P.add("pe", (lambda c_: lambda e: e.transpose(
                    t_ps[:, c_ * 128:(c_ + 1) * 128], x1b[:, c_ * 128:(c_ + 1) * 128], eyeb[:]))(c),
                    reads=["x1b", "eyeb"], writes=(["t_ps"] if c in (0, 7) else []), direct=True)
            P.add("dve", lambda e: e.tensor_copy(out=x1t[:], in_=t_ps[:]),
                  reads=["t_ps"], writes=["x1t"])
            P.wf_flag = True
            for nb in range(2):
                for c in range(8):
                    P.add("pe", (lambda nb_, c_: lambda e: e.matmul(
                        wf_ps[:, nb_ * 512:(nb_ + 1) * 512],
                        x1t[:, c_ * 128:(c_ + 1) * 128],
                        wffn[:, c_ * 1024 + nb_ * 512: c_ * 1024 + (nb_ + 1) * 512],
                        start=(c_ == 0), stop=False))(nb, c),
                        reads=["x1t"] + (wffn_ready if first else []),
                        writes=(["wf"] if (nb, c) == (0, 0) else []))
            P.wf_flag = False

        def emit_epiB1b(p, first):
            P.add("pe", lambda e: e.transpose(t_ps[0:2, 0:128], srow2[:], eyeb[:]),
                  reads=["srow2", "eyeb"], writes=["t_ps"], direct=True)
            P.add("dve", lambda e: e.tensor_copy(out=srt2[:], in_=t_ps[0:2, 0:128]),
                  reads=["t_ps"], writes=["srt2"])
            P.wf_flag = True
            for nb in range(2):
                P.add("pe", (lambda nb_: lambda e: e.matmul(
                    wf_ps[:, nb_ * 512:(nb_ + 1) * 512], srt2[:],
                    corrf[:, nb_ * 512:(nb_ + 1) * 512], start=False, stop=True))(nb),
                    reads=["srt2"] + (corrf_deps if first else []),
                    writes=(["wf"] if nb == 1 else []))
            P.wf_flag = False

        def emit_epiB2(p, ntiles_):
            ps, pi, pbx = p
            pgidx = ps * NT + pi
            P.add("dve", lambda e: e.bn_stats(out=st12[:, 0:6], in_=wf_ps[:, 0:512]),
                  reads=["wf"], writes=["st12"])
            P.add("dve", lambda e: e.bn_stats(out=st12[:, 6:12], in_=wf_ps[:, 512:1024]),
                  reads=["wf"], writes=["st12"])
            P.add("dve", lambda e: e.bn_aggr(out=mv[:, 6:8], in_=st12[:]),
                  reads=["st12"], writes=["mv4"])
            P.add("dve", lambda e: e.tensor_tensor(out=sc[:, 7:8], in0=sc[:, 6:7], in1=sc[:, 6:7],
                                                   op=ALU.mult),
                  reads=["r3"], writes=["r3sq"])
            P.add("dve", lambda e: e.tensor_tensor(out=sc[:, 8:9], in0=mv[:, 7:8], in1=sc[:, 7:8],
                                                   op=ALU.mult),
                  reads=["mv4", "r3sq"], writes=["v4t"])
            P.add("act", lambda e: e.activation(out=sc[:, 9:10], in_=sc[:, 8:9], func=AF.Ln, bias=epsc[:, 0:1]),
                  reads=["epsc", "v4t"], writes=["lnv4"])
            P.add("act", lambda e: e.activation(out=sc[:, 10:11], in_=sc[:, 9:10], func=AF.Exp, scale=-0.5),
                  reads=["lnv4"], writes=["rstd4"])
            P.add("dve", lambda e: e.tensor_tensor(out=sc[:, 11:12], in0=sc[:, 6:7], in1=sc[:, 10:11],
                                                   op=ALU.mult),
                  reads=["r3", "rstd4"], writes=["sA"])
            P.add("dve", lambda e: e.tensor_tensor(out=sc[:, 12:13], in0=mv[:, 6:7], in1=sc[:, 11:12],
                                                   op=ALU.mult),
                  reads=["mv4", "sA"], writes=["mA"])
            X2 = f"x2f{pbx}"
            P.add("dve", (lambda b_: lambda e: e.tensor_scalar(
                out=x2f[b_][:], in0=x1f[:], scalar1=sc[:, 12:13], scalar2=None,
                op0=ALU.subtract))(pbx),
                reads=["x1f", "mA"], writes=[X2])
            P.add("dve", (lambda b_: lambda e: e.scalar_tensor_tensor(
                out=x2f[b_][:], in0=wf_ps[:, 0:1024], scalar=sc[:, 11:12], in1=x2f[b_][:],
                op0=ALU.mult, op1=ALU.add))(pbx),
                reads=["wf", "sA", X2], writes=[X2])
            P.add("sq", (lambda s_, i_, b_: lambda e: e.dma_start(
                out=y_d[s_, i_ * 128:(i_ + 1) * 128, :], in_=x2f[b_][:]))(ps, pi, pbx),
                reads=[X2], writes=[f"yout{pbx}"], dsem=f"yo{pbx}")


        def build_attn_units(s, i):
            gpar = (s * NT + i) % 2
            units_s = []
            units_c = []

            def s_unit(H, j, half):
                def f():
                    if half == 0:
                        P.add("pe", (lambda j_, H2: lambda e: e.matmul(
                            c_ps[:, 0:512],
                            ktv[0:64, H2, j_], qt[gpar][0:64, 4 * H2 * 128:(4 * H2 + 4) * 128],
                            start=True, stop=True, tile_position=(0, 0)))(j, H),
                            reads=[f"kt{j}", f"qt{gpar}"], writes=["cpsA"])
                        P.add("act", (lambda j_: lambda e: e.activation(
                            out=esb[:, j_ * 1024:j_ * 1024 + 512],
                            in_=c_ps[:, 0:512],
                            func=AF.Exp, scale=0.125, bias=cbias[gpar][:, 0:1]))(j),
                            reads=["cpsA", f"cbias{gpar}"], writes=[f"eA{j}"])
                    else:
                        P.add("pe", (lambda j_, H2: lambda e: e.matmul(
                            c_ps[:, 512:1024],
                            ktv[64:128, H2, j_], qt[gpar][64:128, 4 * H2 * 128:(4 * H2 + 4) * 128],
                            start=True, stop=True, tile_position=(64, 0)))(j, H),
                            reads=[f"kt{j}", f"qt{gpar}"], writes=["cpsB"])
                        P.add("act", (lambda j_: lambda e: e.activation(
                            out=esb[:, j_ * 1024 + 512:(j_ + 1) * 1024],
                            in_=c_ps[:, 512:1024],
                            func=AF.Exp, scale=0.125, bias=cbias[gpar][:, 0:1]))(j),
                            reads=["cpsB", f"cbias{gpar}"], writes=[f"eB{j}"])
                        if j == i:
                            P.add("dve", (lambda j_: lambda e: e.scalar_tensor_tensor(
                                out=esb[:, j_ * 1024:(j_ + 1) * 1024],
                                in0=esb[:, j_ * 1024:(j_ + 1) * 1024],
                                scalar=1.0,
                                in1=bass.AP(maskb, 0, [[128, 128], [0, 8], [1, 128]]),
                                op0=ALU.mult, op1=ALU.mult))(i),
                                reads=["maskb", f"eA{i}", f"eB{i}"], writes=[f"eA{i}", f"eB{i}"])
                return f

            def ctx_unit(H, hh):
                def f():
                    grp = hh // 4
                    col = 512 * grp + (hh % 4) * 65
                    for j in ([i] + list(range(i))):
                        P.add("pe", (lambda j_, hh_, H2, col_, i_: lambda e: e.matmul(
                            wf_ps[:, col_:col_ + 65],
                            esb[:, (j_ * 8 + hh_) * 128:(j_ * 8 + hh_ + 1) * 128],
                            vsb[:, j_ * 260 + (2 * H2 + hh_ // 4) * 65:
                                j_ * 260 + (2 * H2 + hh_ // 4) * 65 + 65],
                            start=(j_ == i_), stop=(j_ == (i_ - 1 if i_ else i_))))(j, hh, H, col, i),
                            reads=[("eA" if hh < 4 else "eB") + str(j), f"vs{j}"],
                            writes=(["wf"] if (j == i and hh % 4 == 0) or
                                    (j == (i - 1 if i else 0) and hh % 4 == 3) else []))
                return f

            def fin_unit(H):
                def f():
                    P.add("dve", lambda e: e.reciprocal(
                        out=recipd[:, 0:8],
                        in_=bass.AP(wf_ps, 64, [[1024, 128], [512, 2], [65, 4]])),
                        reads=["wf"], writes=["recipd"])
                    for grp in range(2):
                        P.add("dve", (lambda H2, g_: lambda e: e.scalar_tensor_tensor(
                            out=bass.AP(ctxs, H2 * 512 + g_ * 256, [[1024, 128], [1, 64], [64, 4]]),
                            in0=bass.AP(wf_ps, g_ * 512, [[1024, 128], [1, 64], [65, 4]]),
                            scalar=1.0,
                            in1=bass.AP(recipd, g_ * 4, [[8, 128], [0, 64], [1, 4]]),
                            op0=ALU.mult, op1=ALU.mult))(H, grp),
                            reads=["recipd", "wf"],
                            writes=[f"ctxs{H}_{grp}"])
                    if dbg and (s, i) == (0, 0):
                        P.add("sq", (lambda H2: lambda e: e.dma_start(
                            out=dbg_t["d_e0" if H2 == 0 else "d_e1"], in_=esb[:, 0:1024]))(H),
                            reads=["eA0", "eB0"], dsem="dbg", writes=["dbgchain"])
                return f

            jord = [i] + list(range(i))
            units = []
            ns = 0
            for H in range(2):
                for j in jord:
                    units.append(s_unit(H, j, 0))
                    units.append(s_unit(H, j, 1))
                    if H == 0:
                        ns += 2
                for hh in range(8):
                    units.append(ctx_unit(H, hh))
                units.append(fin_unit(H))
            return units, ns

        for g0 in range(min(3, ntiles)):
            s0, i0 = divmod(g0, NT)
            P.add("sq", (lambda s_, i_, b_: lambda e: e.dma_start(
                out=xf[b_][:], in_=x_d[s_, i_ * 128:(i_ + 1) * 128, :]))(s0, i0, g0 % 3),
                writes=[f"xf{g0 % 3}"], dsem=f"xf{g0 % 3}")
        for g0 in range(min(2, ntiles)):
            s0, i0 = divmod(g0, NT)
            P.add("gq", (lambda s_, i_, b_: lambda e: e.dma_start(
                out=xb[b_][:], in_=x_d[s_, i_ * 128:(i_ + 1) * 128, :]))(s0, i0, g0 % 2),
                writes=[f"xb{g0 % 2}"], dsem=f"xb{g0 % 2}")

        wu = weight_units()

        for k in range(ntiles + 2):
            c, a, e = k, k - 1, k - 2
            if 0 <= a < ntiles:
                A, n_s0 = build_attn_units(*divmod(a, NT))
            else:
                A, n_s0 = [], 0
            epis = []
            if 0 <= e < ntiles:
                es, ei = divmod(e, NT)
                epis = [
                    (lambda es_, ei_, e_: lambda: emit_epiA1((es_, ei_, e_ % 2), first=(e_ == 0)))(es, ei, e),
                    (lambda es_, ei_, e_: lambda: emit_epiA2((es_, ei_, e_ % 2), first=(e_ == 0)))(es, ei, e),
                    (lambda es_, ei_, e_: lambda: emit_epiB1a((es_, ei_, e_ % 2), first=(e_ == 0)))(es, ei, e),
                    (lambda es_, ei_, e_: lambda: emit_epiB1b((es_, ei_, e_ % 2), first=(e_ == 0)))(es, ei, e),
                    (lambda es_, ei_, e_: lambda: emit_epiB2((es_, ei_, e_ % 2), ntiles_=ntiles))(es, ei, e),
                ]
            s_ = max(n_s0, 10)
            bq, pace, Ptail = [], [], []
            if c < ntiles:
                cs, ci = divmod(c, NT)
                def head(c_=c):
                    emit_xt(c_ % 2)
                    if c_ == 0:
                        emit_LN1(0)
                u_p1 = (lambda cs_, ci_, c_: lambda: emit_P1(cs_, ci_, c_ % 2, first=(c_ == 0)))(cs, ci, c)
                u_p1b = (lambda cs_, ci_, c_: lambda: emit_P1b(cs_, ci_, c_ % 2, first=(c_ == 0)))(cs, ci, c)
                u_p2a = (lambda cs_, ci_, c_: lambda: emit_P2a(cs_, ci_, c_ % 2))(cs, ci, c)
                u_p2b = (lambda cs_, ci_, c_: lambda: emit_P2b(cs_, ci_, c_ % 2))(cs, ci, c)
                u_p2c = (lambda cs_, ci_, c_: lambda: emit_P2c(cs_, ci_, c_ % 2))(cs, ci, c)
                if c + 1 < ntiles:
                    Ptail.append((lambda g_: lambda: emit_LN1(g_))(c + 1))
                if c == 0:
                    bq = [(head, False, False), (wu[1], False, False), (wu[0], False, False),
                          (u_p1, False, False), (u_p1b, False, False),
                          (u_p2a, False, False), (u_p2b, False, False), (u_p2c, False, False)]
                    pace = list(range(8))
                    nb_flush = 0
                elif c == 1:
                    bq = [(head, False, False), (wu[2], False, False), (u_p1, False, False),
                          (u_p1b, False, False), (u_p2a, False, False),
                          (u_p2b, False, False), (u_p2c, False, False), (wu[3], False, False)]
                    pace = list(range(7)) + [20]
                    nb_flush = 0
                elif ci == 0:
                    bq = [(head, False, False), (u_p1, True, False),
                          (epis[0], True, False), (epis[1], True, True), (epis[2], True, False),
                          (epis[3], True, False), (epis[4], True, True)]
                    pace = [0, 1, 2, int(.50 * s_), int(.60 * s_), int(.68 * s_), int(.76 * s_)]
                    Ptail = [u_p1b, u_p2a, u_p2b, u_p2c] + Ptail
                    nb_flush = len(bq)
                else:
                    bq = [(head, False, False), (u_p1, True, False), (epis[0], True, False),
                          (u_p1b, True, False), (epis[1], True, True),
                          (epis[2], True, False), (epis[3], True, False), (epis[4], True, True),
                          (u_p2a, True, True), (u_p2b, True, True), (u_p2c, True, False)]
                    pace = [0, 1, 2, 3, int(.50 * s_), int(.58 * s_), int(.64 * s_),
                            int(.68 * s_), int(.72 * s_), int(.80 * s_), int(.88 * s_)]
                    nb_flush = 8
            else:
                bq = [(u, True, True) for u in epis]
                pace = list(range(len(bq)))
                nb_flush = len(bq)
            if not A:
                for u, _, _ in bq:
                    u()
                P.drain_pe()
                for u in Ptail:
                    u()
                continue
            first_ctx = n_s0

            def consume(bi_):
                fn, dfr, ndr = bq[bi_]
                if ndr:
                    P.drain_pe()
                P.defer = dfr
                fn()
                P.defer = False

            bi = 0
            for idx, u in enumerate(A):
                if idx == first_ctx:
                    while bi < nb_flush:
                        consume(bi)
                        bi += 1
                    P.drain_pe(thru_wf=True)
                u()
                if bi < len(bq) and idx >= pace[bi] and (idx < first_ctx or bi >= nb_flush):
                    consume(bi)
                    bi += 1
                P.drain_pe(3)
            while bi < len(bq):
                consume(bi)
                bi += 1
                P.drain_pe(3)
            P.drain_pe()
            for u in Ptail:
                u()
        P.number()

        @block.tensor
        def _(t):
            P.emit("pe", t, sems)

        @block.vector
        def _(v):
            P.emit("dve", v, sems)

        @block.scalar
        def _(s_):
            P.emit("act", s_, sems)

        @block.gpsimd
        def _(g):
            P.emit("gq", g, sems)

        @block.sync
        def _(sy):
            P.emit("sq", sy, sems)
            for kk_ in DMA_KINDS:
                if P.total(kk_):
                    sy.wait_ge(sems[kk_], P.total(kk_))

    return nc


def make_consts():
    eye = np.eye(128, dtype=np.float32)
    ones = np.ones((128, 128), dtype=np.float32)
    kk, qq = np.meshgrid(np.arange(128), np.arange(128), indexing="ij")
    mask = (kk <= qq).astype(np.float32)
    inv_freq = (1.0 / 10000.0 ** (np.arange(0, HD, 2, dtype=np.float64) / HD))
    pos = np.arange(T, dtype=np.float64)
    fr = pos[:, None] * inv_freq[None, :]
    cos = np.cos(fr).astype(np.float32).reshape(NT, 128, 32).transpose(1, 0, 2).reshape(128, 512)
    sin = np.sin(fr).astype(np.float32).reshape(NT, 128, 32).transpose(1, 0, 2).reshape(128, 512)
    return eye, ones, mask, np.ascontiguousarray(cos), np.ascontiguousarray(sin)


def kernel(L0_attn_x, L0_attn_Wq, L0_attn_bq, L0_attn_Wk, L0_attn_bk, L0_attn_Wv,
           L0_attn_bv, L0_attn_Wo, L0_attn_bo, L0_ffn_G1_R, L0_ffn_G2_R,
           L0_ffn_G1_I, L0_ffn_G2_I, L0_ffn_bias_real, L0_ffn_bias_imag):
    x = np.ascontiguousarray(np.asarray(L0_attn_x, dtype=np.float32))
    wqkv = np.ascontiguousarray(np.concatenate(
        [np.asarray(L0_attn_Wq), np.asarray(L0_attn_Wk), np.asarray(L0_attn_Wv)],
        axis=1).astype(np.float32))
    bqkv = np.ascontiguousarray(np.concatenate(
        [np.asarray(L0_attn_bq), np.asarray(L0_attn_bk), np.asarray(L0_attn_bv)]
    ).astype(np.float32).reshape(1, 1536))
    wo = np.ascontiguousarray(np.asarray(L0_attn_Wo, dtype=np.float32))
    bo = np.ascontiguousarray(np.asarray(L0_attn_bo, dtype=np.float32).reshape(1, D))
    bffn = np.ascontiguousarray(np.concatenate(
        [np.asarray(L0_ffn_bias_real), np.asarray(L0_ffn_bias_imag)]
    ).astype(np.float32).reshape(1, D))
    g1r = np.ascontiguousarray(np.asarray(L0_ffn_G1_R, dtype=np.float32).reshape(32, 128))
    g1i = np.ascontiguousarray(np.asarray(L0_ffn_G1_I, dtype=np.float32).reshape(32, 128))
    g2r = np.ascontiguousarray(np.asarray(L0_ffn_G2_R, dtype=np.float32).reshape(64, 16))
    g2i = np.ascontiguousarray(np.asarray(L0_ffn_G2_I, dtype=np.float32).reshape(64, 16))
    eye, onesm, mask, cos, sin = make_consts()

    nc = build_nc()
    in_maps = []
    for c in range(NCORES):
        in_maps.append(dict(
            x=np.ascontiguousarray(x[2 * c:2 * c + 2]),
            wqkv=wqkv, wo=wo, bqkv=bqkv, bo=bo, bffn=bffn,
            g1r=g1r, g1i=g1i, g2r=g2r, g2i=g2i,
            eye=eye, onesm=onesm, maskc=mask, costab=cos, sintab=sin,
        ))
    res = run_bass_kernel_spmd(nc, in_maps, core_ids=list(range(NCORES)))
    out = np.concatenate([np.asarray(r["y"], dtype=np.float32) for r in res.results], axis=0)
    return out.reshape(16, T, D)


# version 22
# speedup vs baseline: 1.0021x; 1.0021x over previous
import numpy as np
import contextlib
import concourse.bass as bass
import concourse.mybir as mybir
from concourse.bass_utils import run_bass_kernel_spmd

F32, BF16 = mybir.dt.float32, mybir.dt.bfloat16
AF = mybir.ActivationFunctionType
ALU = mybir.AluOpType
AX = mybir.AxisListType

D = 1024
T = 2048
NH, NKV, HD = 16, 4, 64
NT = T // 128
NSEQ = 2
EPS = 1e-5
NCORES = 8

ENGINES = ("pe", "dve", "act", "sq", "gq")
DMA_ENGINES = ("sq", "gq")
DMA_KINDS = ("wls", "wlq", "wlo", "wlc", "gb", "rb",
             "wb0", "wb1", "wb2", "wb3", "wb4", "wb5", "wb6", "wb7",
             "xf0", "xf1", "xf2", "xb0", "xb1", "x1b", "yo0", "yo1", "dbg")
CLOSED_KINDS = ("wls", "wlq", "wlo", "wlc", "rb")


class Op:
    __slots__ = ("eng", "fn", "after", "referenced", "sem", "inc_no")

    def __init__(self, eng, fn, after, sem):
        self.eng, self.fn, self.after = eng, fn, after
        self.referenced = False
        self.sem = sem
        self.inc_no = None


class Prog:
    def __init__(self):
        self.ops = {e: [] for e in ENGINES}
        self.res_writer = {}
        self.res_readers = {}
        self.sem_total = {}
        self.pe_fifo = []
        self.defer = False
        self.wf_flag = False

    def add(self, eng, fn, reads=(), writes=(), after=(), dsem=None, direct=False):
        if eng in DMA_ENGINES:
            assert dsem in DMA_KINDS, f"dma op needs a kind, got {dsem}"
            sem = dsem
        else:
            sem = eng
        deps = list(after)
        for r in reads:
            w = self.res_writer.get(r)
            if w is not None:
                deps.append(w)
        for r in writes:
            w = self.res_writer.get(r)
            if w is not None:
                deps.append(w)
            deps.extend(self.res_readers.get(r, ()))
        op = Op(eng, fn, deps, sem)
        if eng == "pe" and self.defer and not direct:
            self.pe_fifo.append((op, self.wf_flag))
        else:
            self.ops[eng].append(op)
        for r in reads:
            self.res_readers.setdefault(r, []).append(op)
        for r in writes:
            self.res_writer[r] = op
            self.res_readers[r] = []
        return op

    def drain_pe(self, n=None, thru_wf=False):
        fifo = self.pe_fifo
        if thru_wf:
            last = -1
            for i_, (op_, wf_) in enumerate(fifo):
                if wf_:
                    last = i_
            take, self.pe_fifo = fifo[:last + 1], fifo[last + 1:]
        elif n is None:
            take, self.pe_fifo = fifo, []
        else:
            take, self.pe_fifo = fifo[:n], fifo[n:]
        for op_, wf_ in take:
            self.ops["pe"].append(op_)

    def number(self):
        for e in ENGINES:
            for op in self.ops[e]:
                for d in op.after:
                    if d is not op:
                        d.referenced = True
        cnt = {}
        for e in ENGINES:
            for op in self.ops[e]:
                if e in DMA_ENGINES:
                    cnt[op.sem] = cnt.get(op.sem, 0) + 16
                    op.inc_no = cnt[op.sem]
                elif op.referenced:
                    cnt[op.sem] = cnt.get(op.sem, 0) + 1
                    op.inc_no = cnt[op.sem]
        self.sem_total = cnt

    def emit(self, e, eng, sems):
        waited = {}
        for op in self.ops[e]:
            need = {}
            for d in op.after:
                tgt = d.inc_no
                if tgt is None:
                    continue
                if d.sem in CLOSED_KINDS:
                    tgt = self.sem_total[d.sem]
                if need.get(d.sem, 0) < tgt:
                    need[d.sem] = tgt
            for s_, tgt in need.items():
                if waited.get(s_, 0) < tgt:
                    eng.wait_ge(sems[s_], tgt)
                    waited[s_] = tgt
            inst = op.fn(eng)
            if op.inc_no is not None and (op.referenced or op.eng in DMA_ENGINES):
                inst.then_inc(sems[op.sem], 16 if op.eng in DMA_ENGINES else 1)

    def total(self, sem):
        return self.sem_total.get(sem, 0)


def build_nc(dbg=False, ntiles=NSEQ * NT):
    NTILES_RT = [ntiles]
    nc = bass.Bass("TRN2", target_bir_lowering=False, debug=False)

    x_d = nc.dram_tensor("x", [NSEQ, T, D], F32, kind="ExternalInput").ap()
    wqkv_d = nc.dram_tensor("wqkv", [D, 1536], F32, kind="ExternalInput").ap()
    wo_d = nc.dram_tensor("wo", [D, D], F32, kind="ExternalInput").ap()
    bqkv_d = nc.dram_tensor("bqkv", [1, 1536], F32, kind="ExternalInput").ap()
    bo_d = nc.dram_tensor("bo", [1, D], F32, kind="ExternalInput").ap()
    bffn_d = nc.dram_tensor("bffn", [1, D], F32, kind="ExternalInput").ap()
    g1r_d = nc.dram_tensor("g1r", [32, 128], F32, kind="ExternalInput").ap()
    g1i_d = nc.dram_tensor("g1i", [32, 128], F32, kind="ExternalInput").ap()
    g2r_d = nc.dram_tensor("g2r", [64, 16], F32, kind="ExternalInput").ap()
    g2i_d = nc.dram_tensor("g2i", [64, 16], F32, kind="ExternalInput").ap()
    eye_d = nc.dram_tensor("eye", [128, 128], F32, kind="ExternalInput").ap()
    ones_d = nc.dram_tensor("onesm", [128, 128], F32, kind="ExternalInput").ap()
    mask_d = nc.dram_tensor("maskc", [128, 128], F32, kind="ExternalInput").ap()
    cos_d = nc.dram_tensor("costab", [128, 512], F32, kind="ExternalInput").ap()
    sin_d = nc.dram_tensor("sintab", [128, 512], F32, kind="ExternalInput").ap()
    y_d = nc.dram_tensor("y", [NSEQ, T, D], F32, kind="ExternalOutput").ap()
    if dbg:
        dbg_t = {
            "d_xt": nc.dram_tensor("d_xt", [128, 1024], BF16, kind="ExternalOutput").ap(),
            "d_rot": nc.dram_tensor("d_rot", [128, 1280], BF16, kind="ExternalOutput").ap(),
            "d_cb": nc.dram_tensor("d_cb", [128, 1], F32, kind="ExternalOutput").ap(),
            "d_e0": nc.dram_tensor("d_e0", [128, 1024], BF16, kind="ExternalOutput").ap(),
            "d_e1": nc.dram_tensor("d_e1", [128, 1024], BF16, kind="ExternalOutput").ap(),
            "d_ctx": nc.dram_tensor("d_ctx", [128, 1024], BF16, kind="ExternalOutput").ap(),
            "d_x1": nc.dram_tensor("d_x1", [128, 1024], F32, kind="ExternalOutput").ap(),
            "d_qt": nc.dram_tensor("d_qt", [128, 1024], BF16, kind="ExternalOutput").ap(),
            "d_wffn": nc.dram_tensor("d_wffn", [128, 8192], BF16, kind="ExternalOutput").ap(),
            "d_corrq": nc.dram_tensor("d_corrq", [2, 1536], BF16, kind="ExternalOutput").ap(),
            "d_corrf": nc.dram_tensor("d_corrf", [2, 1024], BF16, kind="ExternalOutput").ap(),
            "d_vs": nc.dram_tensor("d_vs", [128, 260], BF16, kind="ExternalOutput").ap(),
            "d_kt": nc.dram_tensor("d_kt", [128, 256], BF16, kind="ExternalOutput").ap(),
        }
    d2r_d = nc.dram_tensor("d2r", [512, 512], BF16, kind="Internal")
    d2i_d = nc.dram_tensor("d2i", [512, 512], BF16, kind="Internal")

    with contextlib.ExitStack() as st:
        def sb(name, shape, dt):
            return st.enter_context(nc.sbuf_tensor(name, shape, dt))

        wqkv = sb("wqkv_s", [128, 8 * 1536], BF16)
        wo = sb("wo_s", [128, 8 * 1024], BF16)
        wffn = sb("wffn_s", [128, 8 * 1024], BF16)
        kt = sb("kt_s", [128, 2 * NT * 128], BF16)
        vsb = sb("v_s", [128, NT * 260], BF16)
        esb = sb("e_s", [128, NT * 8 * 128], BF16)
        cost = sb("cos_s", [128, 512], F32)
        sint = sb("sin_s", [128, 512], F32)
        eyef = sb("eyef", [128, 128], F32)
        eyeb = sb("eyeb", [128, 128], BF16)
        maskb = sb("maskb", [128, 128], BF16)
        onesb = sb("onesb", [128, 128], BF16)
        xf = [sb(f"xf{k}", [128, D], F32) for k in range(3)]
        xb = [sb(f"xb{k}", [128, D], BF16) for k in range(2)]
        xt = sb("xt_s", [128, D], BF16)
        x1f = sb("x1f", [128, D], F32)
        x1b = sb("x1b", [128, D], BF16)
        x1t = sb("x1t", [128, D], BF16)
        x2f = [sb(f"x2f{k}", [128, D], F32) for k in range(2)]
        pa = sb("pa", [128, 640], F32)
        pb = sb("pb", [128, 640], F32)
        pc = sb("pcn", [128, 640], F32)
        pd = sb("pd", [128, 640], F32)
        rot = sb("rot", [128, 1280], BF16)
        qt = [sb(f"qt_s{k}", [128, D], BF16) for k in range(2)]
        ctxs = sb("ctxs", [128, D], BF16)
        ctxt = sb("ctxt", [128, D], BF16)
        corrq = sb("corrq", [2, 1536], BF16)
        corrf = sb("corrf", [2, 1024], BF16)
        bob = sb("bob", [1, 1024], BF16)
        srt1 = sb("srt1", [2, 128], BF16)
        srt2 = sb("srt2", [2, 128], BF16)
        st12 = sb("st12", [128, 12], F32)
        mv = sb("mv", [128, 10], F32)
        sc = sb("sc", [128, 16], F32)
        srow1 = sb("srow1", [128, 2], BF16)
        srow1b = sb("srow1b", [128, 2], BF16)
        srow2 = sb("srow2", [128, 2], BF16)
        qkp = sb("qkp", [128, 2], F32)
        qkrow = sb("qkrow", [2, 128], F32)
        qkmax = sb("qkmax", [2, 1], F32)
        prow = sb("prow", [1, 2], F32)
        krun = sb("krun", [1, 1], F32)
        c8n = sb("c8n", [1, 1], BF16)
        cbias = [sb(f"cbias{k}", [128, 1], F32) for k in range(2)]
        epsc = sb("epsc", [128, 1], F32)
        recipd = sb("recipd", [128, 8], F32)
        sqs = sb("sqs", [128, 1024], F32)
        g1t = sb("g1t", [32, 128], BF16)
        g2t = sb("g2t", [64, 16], BF16)
        ta = sb("ta", [128, 32], BF16)
        tg = sb("tg", [16, 64], BF16)
        avec = sb("avec", [4, 1024], BF16)
        bvec = sb("bvec", [4, 256], BF16)
        wst = [sb(f"wst{k}", [128, 256], BF16) for k in range(8)]
        g1u = sb("g1u", [32, 128], BF16)
        g2u = sb("g2u", [64, 16], BF16)

        qkv_ps = st.enter_context(nc.psum_tensor("qkv_ps", [128, 1536], F32))
        wf_ps = st.enter_context(nc.psum_tensor("wf_ps", [128, 1024], F32))
        c_ps = st.enter_context(nc.psum_tensor("c_ps", [128, 1024], F32))
        t_ps = st.enter_context(nc.psum_tensor("t_ps", [128, 1024], BF16))

        ktv = kt[:].rearrange("p (c j t) -> p c j t", c=2, j=NT)

        sems = {e: st.enter_context(nc.semaphore(f"sem_{e}")) for e in ("pe", "dve", "act")}
        for kk_ in DMA_KINDS:
            sems[kk_] = st.enter_context(nc.semaphore(f"sem_{kk_}"))
        block = st.enter_context(nc.Block())

        P = Prog()

        P.add("sq", lambda e: e.dma_start(out=eyef[:], in_=eye_d), writes=["eyef"], dsem="wls")
        P.add("sq", lambda e: e.dma_start(out=cost[:], in_=cos_d), writes=["cos"], dsem="wls")
        P.add("sq", lambda e: e.dma_start(out=sint[:], in_=sin_d), writes=["sin"], dsem="wls")
        P.add("gq", lambda e: e.dma_start(out=eyeb[:], in_=eye_d), writes=["eyeb"], dsem="wlc")
        P.add("gq", lambda e: e.dma_start(out=maskb[:], in_=mask_d), writes=["maskb"], dsem="wlc")
        P.add("gq", lambda e: e.dma_start(out=onesb[:], in_=ones_d), writes=["onesb"], dsem="wlc")
        P.add("gq", lambda e: e.dma_start(out=bob[:], in_=bo_d), writes=["bob"], dsem="wlc")
        P.add("gq", lambda e: e.dma_start(out=corrq[1:2, :], in_=bqkv_d), writes=["corrq_b"], dsem="wlc")
        P.add("gq", lambda e: e.dma_start(out=corrf[1:2, :], in_=bffn_d), writes=["corrf_b"], dsem="wlc")
        P.add("gq", lambda e: e.dma_start(out=g1t[:], in_=g1r_d), writes=["g1"], dsem="wlc")
        P.add("gq", lambda e: e.dma_start(out=g2t[:], in_=g2r_d), writes=["g2"], dsem="wlc")
        P.add("gq", lambda e: e.dma_start(out=g1u[:], in_=g1i_d), writes=["g1i"], dsem="wlc")
        P.add("gq", lambda e: e.dma_start(out=g2u[:], in_=g2i_d), writes=["g2i"], dsem="wlc")
        for c in range(8):
            P.add("gq", (lambda cc: lambda e: e.dma_start(
                out=wqkv[:, cc * 1536:(cc + 1) * 1536],
                in_=wqkv_d[cc * 128:(cc + 1) * 128, :]))(c), writes=[f"wqkv{c}"], dsem="wlq")
        for c in range(8):
            P.add("gq", (lambda cc: lambda e: e.dma_start(
                out=wo[:, cc * 1024:(cc + 1) * 1024],
                in_=wo_d[cc * 128:(cc + 1) * 128, :]))(c), writes=[f"wo{c}"], dsem="wlo")
        P.add("dve", lambda e: e.memset(
            bass.AP(vsb, 64, [[NT * 260, 128], [65, 4 * NT]]), 1.0), writes=["vones"] + [f"vs{j}" for j in range(NT)])
        P.add("dve", lambda e: e.memset(epsc[:], EPS), writes=["epsc"])

        wq_deps = [f"wqkv{c}" for c in range(8)]
        wffn_ready = ([f"wfA{c}" for c in range(4)] + [f"wfB{c}" for c in range(4)] +
                      [f"wfN{c}" for c in range(4)] + [f"wfD{c}" for c in range(4)])

        def weight_units():
            def colsumQ():
                for nb in range(3):
                    for c in range(8):
                        P.add("pe", (lambda nb_, c_: lambda e: e.matmul(
                            c_ps[0:1, 0:512], onesb[:, 0:1],
                            wqkv[:, c_ * 1536 + nb_ * 512: c_ * 1536 + (nb_ + 1) * 512],
                            start=(c_ == 0), stop=(c_ == 7)))(nb, c),
                            reads=wq_deps + ["onesb"],
                            writes=(["cps", "cpsA"] if c in (0, 7) else []))
                    P.add("dve", (lambda nb_: lambda e: e.tensor_copy(
                        out=corrq[0:1, nb_ * 512:(nb_ + 1) * 512], in_=c_ps[0:1, 0:512]))(nb),
                        reads=["cps", "cpsA"], writes=[f"corrq_cs{nb}"])

            def build_half(d2_t, half, gs1, gs2, l1, l2):
                prev = ([f"av{c}" for c in range(4)] + [f"bv{c}" for c in range(4)]) if half else []
                P.add("pe", (lambda g_: lambda e: e.transpose(t_ps[:, 0:32], g_[:], eyeb[0:32, 0:32]))(gs1),
                      reads=[l1, "eyeb"], writes=["t_ps"])
                P.add("dve", (lambda pv: lambda e: e.tensor_copy(out=ta[:], in_=t_ps[:, 0:32]))(prev),
                      reads=["t_ps"] + prev, writes=["ta"])
                P.add("pe", (lambda g_: lambda e: e.transpose(t_ps[0:16, 32:96], g_[:], eyeb[0:64, 0:64]))(gs2),
                      reads=[l2, "eyeb", "ta"], writes=["t_ps"])
                P.add("dve", (lambda pv: lambda e: e.tensor_copy(out=tg[:], in_=t_ps[0:16, 32:96]))(prev),
                      reads=["t_ps"] + prev, writes=["tg"])
                for c in range(4):
                    P.add("sq", (lambda c_: lambda e: e.dma_start(
                        out=avec[c_:c_ + 1, :], in_=ta[c_ * 32:(c_ + 1) * 32, :]))(c),
                        reads=["ta"] + prev, writes=[f"av{c}"], dsem="gb")
                    P.add("sq", (lambda c_: lambda e: e.dma_start(
                        out=bvec[c_:c_ + 1, :], in_=tg[:, c_ * 16:(c_ + 1) * 16]))(c),
                        reads=["tg"] + prev, writes=[f"bv{c}"], dsem="gb")
                vdeps = [f"av{c}" for c in range(4)] + [f"bv{c}" for c in range(4)]
                for m in range(8):
                    poff = 512 + 256 * (m % 2)
                    P.add("pe", (lambda m_, o_: lambda e: e.matmul(
                        c_ps[:, o_:o_ + 256], avec[:, m_ * 128:(m_ + 1) * 128], bvec[:],
                        start=True, stop=True))(m, poff),
                        reads=vdeps, writes=[f"cps{m % 2}w", "cps", "cpsB"])
                    P.add("dve", (lambda b_, o_: lambda e: e.tensor_copy(
                        out=wst[b_][:], in_=c_ps[:, o_:o_ + 256]))(m, poff),
                        reads=[f"cps{m % 2}w", "cpsB"], writes=[f"wst{m}", f"cps{m % 2}w", "cps"])
                    for gh in range(2):
                        for dd in range(4):
                            def mk(m_, gh_, dd_, dt_):
                                def f(e):
                                    in_ap = bass.AP(wst[m_], dd_ * 32 * 256 + gh_ * 128,
                                                    [[256, 32], [16, 8], [1, 16]])
                                    out_ap = bass.AP(dt_, (m_ * 4 + dd_) * 8192 + gh_ * 512 * 8,
                                                     [[16, 32], [512, 8], [1, 16]])
                                    return e.dma_start(out=out_ap, in_=in_ap)
                                return f
                            P.add("sq", mk(m, gh, dd, d2_t),
                                  reads=[f"wst{m}"], writes=[f"d2dma{m}"], dsem=f"wb{m}")

            def h1():
                build_half(d2r_d, 0, g1t, g2t, "g1", "g2")

            def h2():
                build_half(d2i_d, 1, g1u, g2u, "g1i", "g2i")

            def wfin():
                rb_deps = [f"d2dma{m}" for m in range(8)]
                for c in range(4):
                    P.add("sq", (lambda c_: lambda e: e.dma_start(
                        out=wffn[:, c_ * 1024: c_ * 1024 + 512],
                        in_=bass.AP(d2r_d, c_ * 128 * 512, [[512, 128], [1, 512]])))(c),
                        reads=rb_deps, writes=[f"wfA{c}"], dsem="rb")
                    P.add("sq", (lambda c_: lambda e: e.dma_start(
                        out=wffn[:, c_ * 1024 + 512: (c_ + 1) * 1024],
                        in_=bass.AP(d2i_d, c_ * 128 * 512, [[512, 128], [1, 512]])))(c),
                        reads=rb_deps, writes=[f"wfB{c}"], dsem="rb")
                    P.add("sq", (lambda c_: lambda e: e.dma_start(
                        out=wffn[:, (4 + c_) * 1024: (4 + c_) * 1024 + 512],
                        in_=bass.AP(d2i_d, c_ * 128 * 512, [[512, 128], [1, 512]])))(c),
                        reads=rb_deps, writes=[f"wfC{c}"], dsem="rb")
                    P.add("sq", (lambda c_: lambda e: e.dma_start(
                        out=wffn[:, (4 + c_) * 1024 + 512: (5 + c_) * 1024],
                        in_=bass.AP(d2r_d, c_ * 128 * 512, [[512, 128], [1, 512]])))(c),
                        reads=rb_deps, writes=[f"wfD{c}"], dsem="rb")
                for c in range(4):
                    P.add("dve", (lambda c_: lambda e: e.tensor_scalar(
                        out=wffn[:, (4 + c_) * 1024: (4 + c_) * 1024 + 512],
                        in0=wffn[:, (4 + c_) * 1024: (4 + c_) * 1024 + 512],
                        scalar1=-1.0, scalar2=None, op0=ALU.mult))(c),
                        reads=[f"wfC{c}"], writes=[f"wfN{c}"])
                for nb in range(2):
                    for c in range(8):
                        P.add("pe", (lambda nb_, c_: lambda e: e.matmul(
                            c_ps[0:1, 0:512], onesb[:, 0:1],
                            wffn[:, c_ * 1024 + nb_ * 512: c_ * 1024 + (nb_ + 1) * 512],
                            start=(c_ == 0), stop=(c_ == 7)))(nb, c),
                            reads=wffn_ready + ["onesb"],
                            writes=(["cps", "cpsA"] if c in (0, 7) else []))
                    P.add("dve", (lambda nb_: lambda e: e.tensor_copy(
                        out=corrf[0:1, nb_ * 512:(nb_ + 1) * 512], in_=c_ps[0:1, 0:512]))(nb),
                        reads=["cps", "cpsA"], writes=[f"corrf_cs{nb}"])
                if dbg:
                    P.add("sq", lambda e: e.dma_start(out=dbg_t["d_corrf"], in_=corrf[:]),
                          reads=corrf_deps, dsem="dbg", writes=["dbgchain"])
                    P.add("sq", lambda e: e.dma_start(out=dbg_t["d_wffn"], in_=wffn[:]),
                          reads=wffn_ready, dsem="dbg", writes=["dbgchain"])

            return [colsumQ, h1, h2, wfin]

        corrq_deps = ["corrq_cs0", "corrq_cs1", "corrq_cs2", "corrq_b"]
        corrf_deps = ["corrf_cs0", "corrf_cs1", "corrf_b"]


        def emit_LN1(g):
            bxf = g % 3
            par = g % 2
            XF = f"xf{bxf}"
            mv1c = 8 * par
            l1c = 0 + 13 * par
            r1c = 1 + 13 * par
            sr1 = srow1b if par else srow1
            P.add("dve", (lambda x_: lambda e: e.bn_stats(out=st12[:, 0:6], in_=x_[:, 0:512]))(xf[bxf]),
                  reads=[XF], writes=["st12"])
            P.add("dve", (lambda x_: lambda e: e.bn_stats(out=st12[:, 6:12], in_=x_[:, 512:1024]))(xf[bxf]),
                  reads=[XF], writes=["st12"])
            P.add("dve", (lambda mc: lambda e: e.bn_aggr(out=mv[:, mc:mc + 2], in_=st12[:]))(mv1c),
                  reads=["st12"], writes=[f"mv1{par}"])
            P.add("act", (lambda mc, lc: lambda e: e.activation(
                out=sc[:, lc:lc + 1], in_=mv[:, mc + 1:mc + 2], func=AF.Ln, bias=epsc[:, 0:1]))(mv1c, l1c),
                reads=["epsc", f"mv1{par}"], writes=[f"lnv1{par}"])
            P.add("act", (lambda lc, rc: lambda e: e.activation(
                out=sc[:, rc:rc + 1], in_=sc[:, lc:lc + 1], func=AF.Exp, scale=-0.5))(l1c, r1c),
                reads=[f"lnv1{par}"], writes=[f"r1{par}"])
            P.add("act", (lambda lc, s_: lambda e: e.activation(
                out=s_[:, 1:2], in_=sc[:, lc:lc + 1], func=AF.Exp, scale=0.5))(l1c, sr1),
                reads=[f"lnv1{par}"], writes=[f"srow1{par}"])
            P.add("dve", (lambda mc, s_: lambda e: e.tensor_scalar(
                out=s_[:, 0:1], in0=mv[:, mc:mc + 1],
                scalar1=-1.0, scalar2=None, op0=ALU.mult))(mv1c, sr1),
                reads=[f"mv1{par}"], writes=[f"srow1{par}"])

        def emit_epiA1(p, first):
            ps, pi, pbx = p
            for c in range(8):
                hdeps = [f"ctxs{c // 4}_{g_}" for g_ in range(2)]
                P.add("pe", (lambda c_: lambda e: e.transpose(
                    t_ps[:, c_ * 128:(c_ + 1) * 128], ctxs[:, c_ * 128:(c_ + 1) * 128], eyeb[:]))(c),
                    reads=hdeps + ["eyeb"], writes=(["t_ps"] if c in (0, 7) else []), direct=True)
            P.add("dve", lambda e: e.tensor_copy(out=ctxt[:, 0:512], in_=t_ps[:, 0:512]),
                  reads=["t_ps"], writes=["ctxt"])
            P.add("dve", lambda e: e.tensor_copy(out=ctxt[:, 512:1024], in_=t_ps[:, 512:1024]),
                  reads=["t_ps"], writes=["ctxt"])
            P.wf_flag = True
            for nb in range(2):
                for c in range(8):
                    P.add("pe", (lambda nb_, c_: lambda e: e.matmul(
                        wf_ps[:, nb_ * 512:(nb_ + 1) * 512],
                        ctxt[:, c_ * 128:(c_ + 1) * 128],
                        wo[:, c_ * 1024 + nb_ * 512: c_ * 1024 + (nb_ + 1) * 512],
                        start=(c_ == 0), stop=False))(nb, c),
                        reads=["ctxt"] + ([f"wo{cc}" for cc in range(8)] if first else []),
                        writes=(["wf"] if (nb, c) == (0, 0) else []))
                P.add("pe", (lambda nb_: lambda e: e.matmul(
                    wf_ps[:, nb_ * 512:(nb_ + 1) * 512], onesb[0:1, :],
                    bob[0:1, nb_ * 512:(nb_ + 1) * 512], start=False, stop=True))(nb),
                    reads=["bob", "onesb"], writes=(["wf"] if nb == 1 else []))
            P.wf_flag = False

        def emit_epiA2(p, first):
            ps, pi, pbx = p
            pbxf = (ps * NT + pi) % 3
            PXF = f"xf{pbxf}"
            ctx_deps = [f"ctxs{H2}_{g_}" for H2 in range(2) for g_ in range(2)]
            P.add("dve", lambda e: e.bn_stats(out=st12[:, 0:6], in_=wf_ps[:, 0:512]),
                  reads=["wf"], writes=["st12"])
            P.add("dve", lambda e: e.bn_stats(out=st12[:, 6:12], in_=wf_ps[:, 512:1024]),
                  reads=["wf"], writes=["st12"])
            P.add("dve", lambda e: e.bn_aggr(out=mv[:, 2:4], in_=st12[:]),
                  reads=["st12"], writes=["mv2"])
            P.add("act", lambda e: e.activation(out=sc[:, 2:3], in_=mv[:, 3:4], func=AF.Ln, bias=epsc[:, 0:1]),
                  reads=["epsc", "mv2"], writes=["lnv2"])
            P.add("act", lambda e: e.activation(out=sc[:, 3:4], in_=sc[:, 2:3], func=AF.Exp, scale=-0.5),
                  reads=["lnv2"], writes=["r2"])
            P.add("dve", lambda e: e.tensor_tensor(out=sc[:, 4:5], in0=mv[:, 2:3], in1=sc[:, 3:4],
                                                   op=ALU.mult),
                  reads=["mv2", "r2"], writes=["mr2"])
            P.add("dve", (lambda x_: lambda e: e.tensor_scalar(
                out=x1f[:], in0=x_[:], scalar1=sc[:, 4:5], scalar2=None, op0=ALU.subtract))(xf[pbxf]),
                reads=[PXF, "mr2"], writes=["x1f"])
            P.add("dve", lambda e: e.scalar_tensor_tensor(out=x1f[:], in0=wf_ps[:, 0:1024],
                                                          scalar=sc[:, 3:4], in1=x1f[:],
                                                          op0=ALU.mult, op1=ALU.add),
                  reads=["wf", "r2", "x1f"], writes=["x1f"])
            P.add("dve", lambda e: e.tensor_copy(out=x1b[:], in_=x1f[:]),
                  reads=["x1f"], writes=["x1b"])
            P.add("dve", lambda e: e.bn_stats(out=st12[:, 0:6], in_=x1f[:, 0:512]),
                  reads=["x1f"], writes=["st12"])
            P.add("dve", lambda e: e.bn_stats(out=st12[:, 6:12], in_=x1f[:, 512:1024]),
                  reads=["x1f"], writes=["st12"])
            P.add("dve", lambda e: e.bn_aggr(out=mv[:, 4:6], in_=st12[:]),
                  reads=["st12"], writes=["mv3"])
            P.add("act", lambda e: e.activation(out=sc[:, 5:6], in_=mv[:, 5:6], func=AF.Ln, bias=epsc[:, 0:1]),
                  reads=["epsc", "mv3"], writes=["lnv3"])
            P.add("act", lambda e: e.activation(out=sc[:, 6:7], in_=sc[:, 5:6], func=AF.Exp, scale=-0.5),
                  reads=["lnv3"], writes=["r3"])
            P.add("act", lambda e: e.activation(out=srow2[:, 1:2], in_=sc[:, 5:6], func=AF.Exp, scale=0.5),
                  reads=["lnv3"], writes=["srow2"])
            P.add("dve", lambda e: e.tensor_scalar(out=srow2[:, 0:1], in0=mv[:, 4:5],
                                                   scalar1=-1.0, scalar2=None, op0=ALU.mult),
                  reads=["mv3"], writes=["srow2"])
            if dbg and (ps, pi) == (0, 0):
                P.add("sq", lambda e: e.dma_start(out=dbg_t["d_ctx"], in_=ctxs[:]),
                      reads=ctx_deps, dsem="dbg", writes=["dbgchain"])
                P.add("sq", lambda e: e.dma_start(out=dbg_t["d_x1"], in_=x1f[:]),
                      reads=["x1f"], dsem="dbg", writes=["dbgchain"])

        def emit_P1(s, i, bx, first, skip_xt=False):
            par = (s * NT + i) % 2
            sr1 = srow1b if par else srow1
            P.add("pe", (lambda s_: lambda e: e.transpose(t_ps[0:2, 0:128], s_[:], eyeb[:]))(sr1),
                  reads=[f"srow1{par}", "eyeb"], writes=["t_ps"], direct=True)
            P.add("dve", lambda e: e.tensor_copy(out=srt1[:], in_=t_ps[0:2, 0:128]),
                  reads=["t_ps"], writes=["srt1"])
            for nb in range(3):
                P.add("pe", (lambda nb_: lambda e: e.matmul(
                    qkv_ps[:, nb_ * 512:(nb_ + 1) * 512], srt1[:],
                    corrq[:, nb_ * 512:(nb_ + 1) * 512], start=True, stop=False))(nb),
                    reads=["srt1"] + (corrq_deps if first else []),
                    writes=(["qkv"] if nb == 0 else []))
            for nb in range(3):
                for c in range(8):
                    P.add("pe", (lambda nb_, c_: lambda e: e.matmul(
                        qkv_ps[:, nb_ * 512:(nb_ + 1) * 512],
                        xt[:, c_ * 128:(c_ + 1) * 128],
                        wqkv[:, c_ * 1536 + nb_ * 512: c_ * 1536 + (nb_ + 1) * 512],
                        start=False, stop=(c_ == 7)))(nb, c),
                        reads=["xt"] + (wq_deps if first else []),
                        writes=(["qkv"] if (nb, c) == (2, 7) else []))

        def emit_P1b(s, i, bx, first):
            par = (s * NT + i) % 2
            r1c = 1 + 13 * par
            qe = bass.AP(qkv_ps, 0, [[1536, 128], [64, 20], [2, 32]])
            qo = bass.AP(qkv_ps, 1, [[1536, 128], [64, 20], [2, 32]])
            cosv = bass.AP(cost, i * 32, [[512, 128], [0, 20], [1, 32]])
            sinv = bass.AP(sint, i * 32, [[512, 128], [0, 20], [1, 32]])
            for dst, srcv, tab, nm in ((pa, qe, cosv, "pa"), (pb, qo, sinv, "pb"),
                                       (pc, qe, sinv, "pcn"), (pd, qo, cosv, "pd")):
                P.add("dve", (lambda d_, s_, t_, rc: lambda e: e.scalar_tensor_tensor(
                    out=d_[:], in0=s_, scalar=sc[:, rc:rc + 1], in1=t_,
                    op0=ALU.mult, op1=ALU.mult))(dst, srcv, tab, r1c),
                    reads=["qkv", f"r1{par}", "cos", "sin"], writes=[nm])
            for Hh in range(2):
                for off, iA, iB, rA, rB in ((0, pa, pb, "pa", "pb"), (32, pc, pd, "pcn", "pd")):
                    P.add("dve", (lambda H_, o_, a_, b_: lambda e: e.tensor_tensor(
                        out=bass.AP(rot, 512 * H_ + o_, [[1280, 128], [128, 4], [64, 2], [1, 32]]),
                        in0=bass.AP(a_, 256 * H_, [[640, 128], [32, 4], [128, 2], [1, 32]]),
                        in1=bass.AP(b_, 256 * H_, [[640, 128], [32, 4], [128, 2], [1, 32]]),
                        op=ALU.subtract))(Hh, off, iA, iB),
                        reads=[rA, rB], writes=["rot"])
            for off, iA, iB, rA, rB in ((0, pa, pb, "pa", "pb"), (32, pc, pd, "pcn", "pd")):
                P.add("dve", (lambda o_, a_, b_: lambda e: e.tensor_tensor(
                    out=bass.AP(rot, 1024 + o_, [[1280, 128], [64, 4], [1, 32]]),
                    in0=bass.AP(a_, 512, [[640, 128], [32, 4], [1, 32]]),
                    in1=bass.AP(b_, 512, [[640, 128], [32, 4], [1, 32]]),
                    op=ALU.subtract))(off, iA, iB),
                    reads=[rA, rB], writes=["rot"])
            P.add("dve", (lambda i_, rc: lambda e: e.tensor_scalar(
                out=bass.AP(vsb, i_ * 260, [[NT * 260, 128], [65, 4], [1, 64]]),
                in0=bass.AP(qkv_ps, 1280, [[1536, 128], [64, 4], [1, 64]]),
                scalar1=sc[:, rc:rc + 1], scalar2=None, op0=ALU.mult))(i, r1c),
                reads=["qkv", f"r1{par}"] + (["vones"] if first else []), writes=[f"vs{i}"])
        def emit_xt(bx):
            XB = f"xb{bx}"
            for c in range(8):
                P.add("pe", (lambda c_, x_: lambda e: e.transpose(
                    t_ps[:, c_ * 128:(c_ + 1) * 128],
                    x_[:, c_ * 128:(c_ + 1) * 128], eyeb[:]))(c, xb[bx]),
                    reads=[XB, "eyeb"], writes=(["t_ps"] if c in (0, 7) else []))
            P.add("dve", lambda e: e.tensor_copy(out=xt[:], in_=t_ps[:]),
                  reads=["t_ps"], writes=["xt"])

        def emit_P2a(s, i, bx):
            gpar = (s * NT + i) % 2
            P.add("act", lambda e: e.activation(out=sqs[:], in_=rot[:, 0:1024],
                                                func=AF.Square, accum_out=qkp[:, 0:1]),
                  reads=["rot"], writes=["qkp", "sqs"])
            P.add("act", lambda e: e.activation(out=sqs[:, 0:256], in_=rot[:, 1024:1280],
                                                func=AF.Square, accum_out=qkp[:, 1:2]),
                  reads=["rot"], writes=["qkp", "sqs"])
            for cc in range(8):
                P.add("pe", (lambda cc_: lambda e: e.transpose(
                    t_ps[:, cc_ * 128:(cc_ + 1) * 128],
                    rot[:, cc_ * 128:(cc_ + 1) * 128], eyeb[:]))(cc),
                    reads=["rot", "eyeb"], writes=(["t_ps"] if cc in (0, 7) else []))
            P.add("dve", (lambda g_: lambda e: e.tensor_copy(out=qt[g_][:], in_=t_ps[:]))(gpar),
                  reads=["t_ps"], writes=[f"qt{gpar}"])
            for kc in range(2):
                P.add("pe", (lambda kc_: lambda e: e.transpose(
                    t_ps[:, kc_ * 128:(kc_ + 1) * 128],
                    rot[:, 1024 + kc_ * 128:1024 + (kc_ + 1) * 128], eyeb[:]))(kc),
                    reads=["rot", "eyeb"], writes=["t_ps"])
            P.add("dve", (lambda i_: lambda e: e.tensor_copy(
                out=bass.AP(kt, i_ * 128, [[2 * NT * 128, 128], [NT * 128, 2], [1, 128]]),
                in_=t_ps[:, 0:256].rearrange("p (c t) -> p c t", c=2)))(i),
                reads=["t_ps"], writes=[f"kt{i}"])
            P.add("pe", lambda e: e.transpose(qkv_ps[0:2, 1400:1528], qkp[:], eyef[:]),
                  reads=["qkp", "eyef"], writes=["qkv"])
            P.add("dve", lambda e: e.tensor_copy(out=qkrow[:], in_=qkv_ps[0:2, 1400:1528]),
                  reads=["qkv"], writes=["qkrow"])
            P.add("dve", lambda e: e.tensor_reduce(out=qkmax[:], in_=qkrow[:], axis=AX.X, op=ALU.max),
                  reads=["qkrow"], writes=["qkmax"])
            P.add("pe", lambda e: e.transpose(qkv_ps[0:1, 1398:1400], qkmax[:], eyef[0:2, 0:2]),
                  reads=["qkmax", "eyef"], writes=["qkv"])
            P.add("dve", lambda e: e.tensor_copy(out=prow[:], in_=qkv_ps[0:1, 1398:1400]),
                  reads=["qkv"], writes=["prow"])

        def emit_P2b(s, i, bx):
            gpar = (s * NT + i) % 2
            if i == 0:
                P.add("dve", lambda e: e.tensor_copy(out=krun[:], in_=prow[0:1, 1:2]),
                      reads=["prow"], writes=["krun"])
            else:
                P.add("dve", lambda e: e.tensor_tensor(out=krun[:], in0=krun[:],
                                                       in1=prow[0:1, 1:2], op=ALU.max),
                      reads=["prow"], writes=["krun"])
            P.add("dve", lambda e: e.tensor_tensor(out=prow[0:1, 0:1], in0=prow[0:1, 0:1],
                                                   in1=krun[:], op=ALU.mult),
                  reads=["krun"], writes=["prow"])
            P.add("act", lambda e: e.activation(out=prow[0:1, 1:2], in_=prow[0:1, 0:1],
                                                func=AF.Ln, bias=epsc[0:1, 0:1]),
                  reads=["prow", "epsc"], writes=["prowln"])
            P.add("act", lambda e: e.activation(out=prow[0:1, 0:1], in_=prow[0:1, 1:2],
                                                func=AF.Exp, scale=0.5),
                  reads=["prowln"], writes=["prow"])
            P.add("dve", lambda e: e.tensor_scalar(out=c8n[:], in0=prow[0:1, 0:1],
                                                   scalar1=-0.125, scalar2=None, op0=ALU.mult),
                  reads=["prow"], writes=["c8n"])

        def emit_P2c(s, i, bx):
            gpar = (s * NT + i) % 2
            P.add("pe", lambda e: e.matmul(qkv_ps[:, 1535:1536], onesb[0:1, :], c8n[:],
                                           start=True, stop=True),
                  reads=["c8n", "onesb"], writes=["qkv"]),
            P.add("dve", (lambda g_: lambda e: e.tensor_copy(out=cbias[g_][:], in_=qkv_ps[:, 1535:1536]))(gpar),
                  reads=["qkv"], writes=[f"cbias{gpar}"])
            gidx_ = s * NT + i
            if gidx_ + 2 < NTILES_RT[0]:
                s2, i2 = divmod(gidx_ + 2, NT)
                P.add("gq", (lambda s_, i_, b_: lambda e: e.dma_start(
                    out=xb[b_][:], in_=x_d[s_, i_ * 128:(i_ + 1) * 128, :]))(s2, i2, gpar),
                    writes=[f"xb{gpar}"], dsem=f"xb{gpar}")
            if dbg and (s, i) == (0, 0):
                P.add("sq", lambda e: e.dma_start(out=dbg_t["d_xt"], in_=xt[:]), reads=["xt"], dsem="dbg", writes=["dbgchain"])
                P.add("sq", lambda e: e.dma_start(out=dbg_t["d_rot"], in_=rot[:]), reads=["rot"], dsem="dbg", writes=["dbgchain"])
                P.add("sq", lambda e: e.dma_start(out=dbg_t["d_cb"], in_=cbias[0][:]), reads=["cbias0"], dsem="dbg", writes=["dbgchain"])
                P.add("sq", lambda e: e.dma_start(out=dbg_t["d_qt"], in_=qt[0][:]), reads=["qt0"], dsem="dbg", writes=["dbgchain"])
                P.add("sq", lambda e: e.dma_start(out=dbg_t["d_vs"], in_=vsb[:, 0:260]), reads=["vs0"], dsem="dbg", writes=["dbgchain"])
                P.add("sq", lambda e: e.dma_start(out=dbg_t["d_kt"][:, 0:128], in_=kt[:, 0:128]), reads=["kt0"], dsem="dbg", writes=["dbgchain"])
                P.add("sq", lambda e: e.dma_start(out=dbg_t["d_kt"][:, 128:256], in_=kt[:, 2048:2176]), reads=["kt0"], dsem="dbg", writes=["dbgchain"])
                P.add("sq", lambda e: e.dma_start(out=dbg_t["d_corrq"], in_=corrq[:]),
                      reads=corrq_deps, dsem="dbg", writes=["dbgchain"])

        def emit_epiB1a(p, first):
            ps, pi, pbx = p
            pg = ps * NT + pi
            if pg + 3 < NTILES_RT[0]:
                s3, i3 = divmod(pg + 3, NT)
                b3 = (pg + 3) % 3
                P.add("sq", (lambda s_, i_, b_: lambda e: e.dma_start(
                    out=xf[b_][:], in_=x_d[s_, i_ * 128:(i_ + 1) * 128, :]))(s3, i3, b3),
                    writes=[f"xf{b3}"], dsem=f"xf{b3}")
            for c in range(8):
                P.add("pe", (lambda c_: lambda e: e.transpose(
                    t_ps[:, c_ * 128:(c_ + 1) * 128], x1b[:, c_ * 128:(c_ + 1) * 128], eyeb[:]))(c),
                    reads=["x1b", "eyeb"], writes=(["t_ps"] if c in (0, 7) else []), direct=True)
            P.add("dve", lambda e: e.tensor_copy(out=x1t[:], in_=t_ps[:]),
                  reads=["t_ps"], writes=["x1t"])
            P.wf_flag = True
            for nb in range(2):
                for c in range(8):
                    P.add("pe", (lambda nb_, c_: lambda e: e.matmul(
                        wf_ps[:, nb_ * 512:(nb_ + 1) * 512],
                        x1t[:, c_ * 128:(c_ + 1) * 128],
                        wffn[:, c_ * 1024 + nb_ * 512: c_ * 1024 + (nb_ + 1) * 512],
                        start=(c_ == 0), stop=False))(nb, c),
                        reads=["x1t"] + (wffn_ready if first else []),
                        writes=(["wf"] if (nb, c) == (0, 0) else []))
            P.wf_flag = False

        def emit_epiB1b(p, first):
            P.add("pe", lambda e: e.transpose(t_ps[0:2, 0:128], srow2[:], eyeb[:]),
                  reads=["srow2", "eyeb"], writes=["t_ps"], direct=True)
            P.add("dve", lambda e: e.tensor_copy(out=srt2[:], in_=t_ps[0:2, 0:128]),
                  reads=["t_ps"], writes=["srt2"])
            P.wf_flag = True
            for nb in range(2):
                P.add("pe", (lambda nb_: lambda e: e.matmul(
                    wf_ps[:, nb_ * 512:(nb_ + 1) * 512], srt2[:],
                    corrf[:, nb_ * 512:(nb_ + 1) * 512], start=False, stop=True))(nb),
                    reads=["srt2"] + (corrf_deps if first else []),
                    writes=(["wf"] if nb == 1 else []))
            P.wf_flag = False

        def emit_epiB2(p, ntiles_):
            ps, pi, pbx = p
            pgidx = ps * NT + pi
            P.add("dve", lambda e: e.bn_stats(out=st12[:, 0:6], in_=wf_ps[:, 0:512]),
                  reads=["wf"], writes=["st12"])
            P.add("dve", lambda e: e.bn_stats(out=st12[:, 6:12], in_=wf_ps[:, 512:1024]),
                  reads=["wf"], writes=["st12"])
            P.add("dve", lambda e: e.bn_aggr(out=mv[:, 6:8], in_=st12[:]),
                  reads=["st12"], writes=["mv4"])
            P.add("dve", lambda e: e.tensor_tensor(out=sc[:, 7:8], in0=sc[:, 6:7], in1=sc[:, 6:7],
                                                   op=ALU.mult),
                  reads=["r3"], writes=["r3sq"])
            P.add("dve", lambda e: e.tensor_tensor(out=sc[:, 8:9], in0=mv[:, 7:8], in1=sc[:, 7:8],
                                                   op=ALU.mult),
                  reads=["mv4", "r3sq"], writes=["v4t"])
            P.add("act", lambda e: e.activation(out=sc[:, 9:10], in_=sc[:, 8:9], func=AF.Ln, bias=epsc[:, 0:1]),
                  reads=["epsc", "v4t"], writes=["lnv4"])
            P.add("act", lambda e: e.activation(out=sc[:, 10:11], in_=sc[:, 9:10], func=AF.Exp, scale=-0.5),
                  reads=["lnv4"], writes=["rstd4"])
            P.add("dve", lambda e: e.tensor_tensor(out=sc[:, 11:12], in0=sc[:, 6:7], in1=sc[:, 10:11],
                                                   op=ALU.mult),
                  reads=["r3", "rstd4"], writes=["sA"])
            P.add("dve", lambda e: e.tensor_tensor(out=sc[:, 12:13], in0=mv[:, 6:7], in1=sc[:, 11:12],
                                                   op=ALU.mult),
                  reads=["mv4", "sA"], writes=["mA"])
            X2 = f"x2f{pbx}"
            P.add("dve", (lambda b_: lambda e: e.tensor_scalar(
                out=x2f[b_][:], in0=x1f[:], scalar1=sc[:, 12:13], scalar2=None,
                op0=ALU.subtract))(pbx),
                reads=["x1f", "mA"], writes=[X2])
            P.add("dve", (lambda b_: lambda e: e.scalar_tensor_tensor(
                out=x2f[b_][:], in0=wf_ps[:, 0:1024], scalar=sc[:, 11:12], in1=x2f[b_][:],
                op0=ALU.mult, op1=ALU.add))(pbx),
                reads=["wf", "sA", X2], writes=[X2])
            P.add("sq", (lambda s_, i_, b_: lambda e: e.dma_start(
                out=y_d[s_, i_ * 128:(i_ + 1) * 128, :], in_=x2f[b_][:]))(ps, pi, pbx),
                reads=[X2], writes=[f"yout{pbx}"], dsem=f"yo{pbx}")


        def build_attn_units(s, i):
            gpar = (s * NT + i) % 2
            units_s = []
            units_c = []

            def s_unit(H, j, half):
                def f():
                    if half == 0:
                        P.add("pe", (lambda j_, H2: lambda e: e.matmul(
                            c_ps[:, 0:512],
                            ktv[0:64, H2, j_], qt[gpar][0:64, 4 * H2 * 128:(4 * H2 + 4) * 128],
                            start=True, stop=True, tile_position=(0, 0)))(j, H),
                            reads=[f"kt{j}", f"qt{gpar}"], writes=["cpsA"])
                        P.add("act", (lambda j_: lambda e: e.activation(
                            out=esb[:, j_ * 1024:j_ * 1024 + 512],
                            in_=c_ps[:, 0:512],
                            func=AF.Exp, scale=0.125, bias=cbias[gpar][:, 0:1]))(j),
                            reads=["cpsA", f"cbias{gpar}"], writes=[f"eA{j}"])
                    else:
                        P.add("pe", (lambda j_, H2: lambda e: e.matmul(
                            c_ps[:, 512:1024],
                            ktv[64:128, H2, j_], qt[gpar][64:128, 4 * H2 * 128:(4 * H2 + 4) * 128],
                            start=True, stop=True, tile_position=(64, 0)))(j, H),
                            reads=[f"kt{j}", f"qt{gpar}"], writes=["cpsB"])
                        P.add("act", (lambda j_: lambda e: e.activation(
                            out=esb[:, j_ * 1024 + 512:(j_ + 1) * 1024],
                            in_=c_ps[:, 512:1024],
                            func=AF.Exp, scale=0.125, bias=cbias[gpar][:, 0:1]))(j),
                            reads=["cpsB", f"cbias{gpar}"], writes=[f"eB{j}"])
                        if j == i:
                            P.add("dve", (lambda j_: lambda e: e.scalar_tensor_tensor(
                                out=esb[:, j_ * 1024:(j_ + 1) * 1024],
                                in0=esb[:, j_ * 1024:(j_ + 1) * 1024],
                                scalar=1.0,
                                in1=bass.AP(maskb, 0, [[128, 128], [0, 8], [1, 128]]),
                                op0=ALU.mult, op1=ALU.mult))(i),
                                reads=["maskb", f"eA{i}", f"eB{i}"], writes=[f"eA{i}", f"eB{i}"])
                return f

            def ctx_unit(H, hh):
                def f():
                    grp = hh // 4
                    col = 512 * grp + (hh % 4) * 65
                    for j in ([i] + list(range(i))):
                        P.add("pe", (lambda j_, hh_, H2, col_, i_: lambda e: e.matmul(
                            wf_ps[:, col_:col_ + 65],
                            esb[:, (j_ * 8 + hh_) * 128:(j_ * 8 + hh_ + 1) * 128],
                            vsb[:, j_ * 260 + (2 * H2 + hh_ // 4) * 65:
                                j_ * 260 + (2 * H2 + hh_ // 4) * 65 + 65],
                            start=(j_ == i_), stop=(j_ == (i_ - 1 if i_ else i_))))(j, hh, H, col, i),
                            reads=[("eA" if hh < 4 else "eB") + str(j), f"vs{j}"],
                            writes=(["wf"] if (j == i and hh % 4 == 0) or
                                    (j == (i - 1 if i else 0) and hh % 4 == 3) else []))
                return f

            def fin_unit(H):
                def f():
                    P.add("dve", lambda e: e.reciprocal(
                        out=recipd[:, 0:8],
                        in_=bass.AP(wf_ps, 64, [[1024, 128], [512, 2], [65, 4]])),
                        reads=["wf"], writes=["recipd"])
                    for grp in range(2):
                        P.add("dve", (lambda H2, g_: lambda e: e.scalar_tensor_tensor(
                            out=bass.AP(ctxs, H2 * 512 + g_ * 256, [[1024, 128], [1, 64], [64, 4]]),
                            in0=bass.AP(wf_ps, g_ * 512, [[1024, 128], [1, 64], [65, 4]]),
                            scalar=1.0,
                            in1=bass.AP(recipd, g_ * 4, [[8, 128], [0, 64], [1, 4]]),
                            op0=ALU.mult, op1=ALU.mult))(H, grp),
                            reads=["recipd", "wf"],
                            writes=[f"ctxs{H}_{grp}"])
                    if dbg and (s, i) == (0, 0):
                        P.add("sq", (lambda H2: lambda e: e.dma_start(
                            out=dbg_t["d_e0" if H2 == 0 else "d_e1"], in_=esb[:, 0:1024]))(H),
                            reads=["eA0", "eB0"], dsem="dbg", writes=["dbgchain"])
                return f

            jord = [i] + list(range(i))
            units = []
            ns = 0
            for H in range(2):
                for j in jord:
                    units.append(s_unit(H, j, 0))
                    units.append(s_unit(H, j, 1))
                    if H == 0:
                        ns += 2
                for hh in range(8):
                    units.append(ctx_unit(H, hh))
                units.append(fin_unit(H))
            return units, ns

        for g0 in range(min(3, ntiles)):
            s0, i0 = divmod(g0, NT)
            P.add("sq", (lambda s_, i_, b_: lambda e: e.dma_start(
                out=xf[b_][:], in_=x_d[s_, i_ * 128:(i_ + 1) * 128, :]))(s0, i0, g0 % 3),
                writes=[f"xf{g0 % 3}"], dsem=f"xf{g0 % 3}")
        for g0 in range(min(2, ntiles)):
            s0, i0 = divmod(g0, NT)
            P.add("gq", (lambda s_, i_, b_: lambda e: e.dma_start(
                out=xb[b_][:], in_=x_d[s_, i_ * 128:(i_ + 1) * 128, :]))(s0, i0, g0 % 2),
                writes=[f"xb{g0 % 2}"], dsem=f"xb{g0 % 2}")

        wu = weight_units()

        for k in range(ntiles + 2):
            c, a, e = k, k - 1, k - 2
            if 0 <= a < ntiles:
                A, n_s0 = build_attn_units(*divmod(a, NT))
            else:
                A, n_s0 = [], 0
            epis = []
            if 0 <= e < ntiles:
                es, ei = divmod(e, NT)
                epis = [
                    (lambda es_, ei_, e_: lambda: emit_epiA1((es_, ei_, e_ % 2), first=(e_ == 0)))(es, ei, e),
                    (lambda es_, ei_, e_: lambda: emit_epiA2((es_, ei_, e_ % 2), first=(e_ == 0)))(es, ei, e),
                    (lambda es_, ei_, e_: lambda: emit_epiB1a((es_, ei_, e_ % 2), first=(e_ == 0)))(es, ei, e),
                    (lambda es_, ei_, e_: lambda: emit_epiB1b((es_, ei_, e_ % 2), first=(e_ == 0)))(es, ei, e),
                    (lambda es_, ei_, e_: lambda: emit_epiB2((es_, ei_, e_ % 2), ntiles_=ntiles))(es, ei, e),
                ]
            s_ = max(n_s0, 10)
            bq, pace, Ptail = [], [], []
            if c < ntiles:
                cs, ci = divmod(c, NT)
                def head(c_=c):
                    emit_xt(c_ % 2)
                    if c_ == 0:
                        emit_LN1(0)
                u_p1 = (lambda cs_, ci_, c_: lambda: emit_P1(cs_, ci_, c_ % 2, first=(c_ == 0)))(cs, ci, c)
                u_p1b = (lambda cs_, ci_, c_: lambda: emit_P1b(cs_, ci_, c_ % 2, first=(c_ == 0)))(cs, ci, c)
                u_p2a = (lambda cs_, ci_, c_: lambda: emit_P2a(cs_, ci_, c_ % 2))(cs, ci, c)
                u_p2b = (lambda cs_, ci_, c_: lambda: emit_P2b(cs_, ci_, c_ % 2))(cs, ci, c)
                u_p2c = (lambda cs_, ci_, c_: lambda: emit_P2c(cs_, ci_, c_ % 2))(cs, ci, c)
                u_ln_next = None
                if c + 1 < ntiles:
                    u_ln_next = (lambda g_: lambda: emit_LN1(g_))(c + 1)
                if u_ln_next is not None and c < 2:
                    Ptail.append(u_ln_next)
                if c == 0:
                    bq = [(head, False, False), (wu[1], False, False), (wu[0], False, False),
                          (u_p1, False, False), (u_p1b, False, False),
                          (u_p2a, False, False), (u_p2b, False, False), (u_p2c, False, False)]
                    pace = list(range(8))
                    nb_flush = 0
                elif c == 1:
                    bq = [(head, False, False), (wu[2], False, False), (u_p1, False, False),
                          (u_p1b, False, False), (u_p2a, False, False),
                          (u_p2b, False, False), (u_p2c, False, False), (wu[3], False, False)]
                    pace = list(range(7)) + [20]
                    nb_flush = 0
                elif ci == 0:
                    bq = [(head, False, False), (u_p1, True, False),
                          (epis[0], True, False), (epis[1], True, True), (epis[2], True, False),
                          (epis[3], True, False), (epis[4], True, True)]
                    pace = [0, 1, 2, int(.50 * s_), int(.60 * s_), int(.68 * s_), int(.76 * s_)]
                    if u_ln_next is not None:
                        Ptail.append(u_ln_next)
                    Ptail = [u_p1b, u_p2a, u_p2b, u_p2c] + Ptail
                    nb_flush = len(bq)
                else:
                    bq = [(head, False, False), (u_p1, True, False), (epis[0], True, False),
                          (u_p1b, True, False), (epis[1], True, True),
                          (u_ln_next if u_ln_next is not None else (lambda: None), False, False),
                          (epis[2], True, False), (epis[3], True, False), (epis[4], True, True),
                          (u_p2a, True, True), (u_p2b, True, True), (u_p2c, True, False)]
                    pace = [0, 1, 2, 3, int(.50 * s_), int(.56 * s_), int(.66 * s_),
                            int(.72 * s_), int(.76 * s_), int(.80 * s_), int(.86 * s_), int(.92 * s_)]
                    nb_flush = 9
            else:
                bq = [(u, True, True) for u in epis]
                pace = list(range(len(bq)))
                nb_flush = len(bq)
            if not A:
                for u, _, _ in bq:
                    u()
                P.drain_pe()
                for u in Ptail:
                    u()
                continue
            first_ctx = n_s0

            def consume(bi_):
                fn, dfr, ndr = bq[bi_]
                if ndr:
                    P.drain_pe()
                P.defer = dfr
                fn()
                P.defer = False

            bi = 0
            for idx, u in enumerate(A):
                if idx == first_ctx:
                    while bi < nb_flush:
                        consume(bi)
                        bi += 1
                    P.drain_pe(thru_wf=True)
                u()
                if bi < len(bq) and idx >= pace[bi] and (idx < first_ctx or bi >= nb_flush):
                    consume(bi)
                    bi += 1
                P.drain_pe(3)
            while bi < len(bq):
                consume(bi)
                bi += 1
                P.drain_pe(3)
            P.drain_pe()
            for u in Ptail:
                u()
        P.number()

        @block.tensor
        def _(t):
            P.emit("pe", t, sems)

        @block.vector
        def _(v):
            P.emit("dve", v, sems)

        @block.scalar
        def _(s_):
            P.emit("act", s_, sems)

        @block.gpsimd
        def _(g):
            P.emit("gq", g, sems)

        @block.sync
        def _(sy):
            P.emit("sq", sy, sems)
            for kk_ in DMA_KINDS:
                if P.total(kk_):
                    sy.wait_ge(sems[kk_], P.total(kk_))

    return nc


def make_consts():
    eye = np.eye(128, dtype=np.float32)
    ones = np.ones((128, 128), dtype=np.float32)
    kk, qq = np.meshgrid(np.arange(128), np.arange(128), indexing="ij")
    mask = (kk <= qq).astype(np.float32)
    inv_freq = (1.0 / 10000.0 ** (np.arange(0, HD, 2, dtype=np.float64) / HD))
    pos = np.arange(T, dtype=np.float64)
    fr = pos[:, None] * inv_freq[None, :]
    cos = np.cos(fr).astype(np.float32).reshape(NT, 128, 32).transpose(1, 0, 2).reshape(128, 512)
    sin = np.sin(fr).astype(np.float32).reshape(NT, 128, 32).transpose(1, 0, 2).reshape(128, 512)
    return eye, ones, mask, np.ascontiguousarray(cos), np.ascontiguousarray(sin)


def kernel(L0_attn_x, L0_attn_Wq, L0_attn_bq, L0_attn_Wk, L0_attn_bk, L0_attn_Wv,
           L0_attn_bv, L0_attn_Wo, L0_attn_bo, L0_ffn_G1_R, L0_ffn_G2_R,
           L0_ffn_G1_I, L0_ffn_G2_I, L0_ffn_bias_real, L0_ffn_bias_imag):
    x = np.ascontiguousarray(np.asarray(L0_attn_x, dtype=np.float32))
    wqkv = np.ascontiguousarray(np.concatenate(
        [np.asarray(L0_attn_Wq), np.asarray(L0_attn_Wk), np.asarray(L0_attn_Wv)],
        axis=1).astype(np.float32))
    bqkv = np.ascontiguousarray(np.concatenate(
        [np.asarray(L0_attn_bq), np.asarray(L0_attn_bk), np.asarray(L0_attn_bv)]
    ).astype(np.float32).reshape(1, 1536))
    wo = np.ascontiguousarray(np.asarray(L0_attn_Wo, dtype=np.float32))
    bo = np.ascontiguousarray(np.asarray(L0_attn_bo, dtype=np.float32).reshape(1, D))
    bffn = np.ascontiguousarray(np.concatenate(
        [np.asarray(L0_ffn_bias_real), np.asarray(L0_ffn_bias_imag)]
    ).astype(np.float32).reshape(1, D))
    g1r = np.ascontiguousarray(np.asarray(L0_ffn_G1_R, dtype=np.float32).reshape(32, 128))
    g1i = np.ascontiguousarray(np.asarray(L0_ffn_G1_I, dtype=np.float32).reshape(32, 128))
    g2r = np.ascontiguousarray(np.asarray(L0_ffn_G2_R, dtype=np.float32).reshape(64, 16))
    g2i = np.ascontiguousarray(np.asarray(L0_ffn_G2_I, dtype=np.float32).reshape(64, 16))
    eye, onesm, mask, cos, sin = make_consts()

    nc = build_nc()
    in_maps = []
    for c in range(NCORES):
        in_maps.append(dict(
            x=np.ascontiguousarray(x[2 * c:2 * c + 2]),
            wqkv=wqkv, wo=wo, bqkv=bqkv, bo=bo, bffn=bffn,
            g1r=g1r, g1i=g1i, g2r=g2r, g2i=g2i,
            eye=eye, onesm=onesm, maskc=mask, costab=cos, sintab=sin,
        ))
    res = run_bass_kernel_spmd(nc, in_maps, core_ids=list(range(NCORES)))
    out = np.concatenate([np.asarray(r["y"], dtype=np.float32) for r in res.results], axis=0)
    return out.reshape(16, T, D)
